# version 03
# baseline (speedup 1.0000x reference)
import numpy as np
from contextlib import ExitStack

import concourse.bass as bass
import concourse.mybir as mybir
from concourse.bass_utils import run_bass_kernel_spmd

F32 = mybir.dt.float32
BF16 = mybir.dt.bfloat16
U8 = mybir.dt.uint8
AF = mybir.ActivationFunctionType
ALU = mybir.AluOpType
AX = mybir.AxisListType

NCORES = 8
T = 2048
D = 1024
DC = D // 128
FT = 512
NF = T // FT
RC = 12582912.0
QMAX = 32767.0
EPS = 1e-5
KB = 1024

ENGINES = ("sync", "gpsimd", "vector", "scalar", "tensor")


class PB:
    SERIAL = False

    def __init__(self, arena_kb=196):
        self.nc = bass.Bass("TRN2", target_bir_lowering=False)
        self.es = ExitStack()
        self.q = {e: [] for e in ENGINES}
        self.semh = {}
        self.semn = {}
        self.waited = {e: {} for e in ENGINES}
        self.arena_size = arena_kb * KB
        self._placed = []
        self.bank_free = {}
        self.bfree = {}
        for s in ("LD", "ST", "WD", "WD0", "WD1", "V", "S", "T", "G"):
            self.sem(s)

    def sem(self, name):
        self.semh[name] = self.es.enter_context(self.nc.semaphore(name))
        self.semn[name] = 0
        return name

    def place(self, name, shape, dt, off_bytes):
        per_part = int(np.prod(shape[1:])) * mybir.dt.size(dt)
        assert off_bytes + per_part <= self.arena_size, (name, off_bytes, per_part)
        h = self.nc.virtual_sbuf_tensor(name, list(shape), dt)
        self._placed.append((h, off_bytes))
        return h

    def psum(self, name, shape=(128, FT), dt=F32):
        return self.es.enter_context(self.nc.psum_tensor(name, list(shape), dt))

    def din(self, name, shape, dt=F32):
        return self.nc.dram_tensor(name, shape, dt, kind="ExternalInput").ap()

    def dout(self, name, shape, dt=F32):
        return self.nc.dram_tensor(name, shape, dt, kind="ExternalOutput").ap()

    def emit(self, eng, fn, deps=(), inc=None, amt=1):
        w = self.waited[eng]
        if PB.SERIAL:
            deps = list(deps) + [(sname, cnt) for sname, cnt in self.semn.items() if cnt > 0]
        for dep in deps:
            if dep is None:
                continue
            s, v = dep
            if w.get(s, 0) < v:
                w[s] = v
                self.q[eng].append(lambda e, s=s, v=v: e.wait_ge(self.semh[s], v))
        if inc is not None:
            self.semn[inc] += amt
            tok = (inc, self.semn[inc])
            self.q[eng].append(
                lambda e, fn=fn, s=inc, a=amt: fn(e).then_inc(self.semh[s], a))
            return tok
        self.q[eng].append(lambda e, fn=fn: fn(e))
        return None

    def dma(self, eng, out, in_, deps=(), sem="LD"):
        return self.emit(eng, lambda e, o=out, i=in_: e.dma_start(out=o, in_=i),
                         deps, inc=sem, amt=16)

    def finalize(self):
        arena = self.es.enter_context(
            self.nc.sbuf_tensor("arena", [128, self.arena_size], U8))
        base = self.nc.lookup_mloc(arena).addr
        for h, off in self._placed:
            self.nc._tensor(h.name, h.shape, h.dtype, type="SB")
            mloc = self.nc.lookup_mloc(h)
            mloc.allocated = True
            mloc.addr = base + off
        blk = self.es.enter_context(self.nc.Block())
        for e in ENGINES:
            fns = self.q[e]
            if not fns:
                continue
            def body(h, fns=fns):
                for f in fns:
                    f(h)
            getattr(blk, e)(body)
        return self.nc

    def make_cpool(self, fx):
        self.cpool = fx("cpool", [128, 8], F32)
        self._cvals = {}

    def cval(self, v):
        if v not in self._cvals:
            j = len(self._cvals)
            tok = self.emit("vector", lambda e, j=j, v=v: e.memset(self.cpool[:, j:j + 1], v), inc="V")
            self._cvals[v] = (j, tok)
        j, tok = self._cvals[v]
        return self.cpool[:, j:j + 1], tok

    def f(self, key):
        return self.bfree.get(key)

    def setf(self, key, tok):
        self.bfree[key] = tok

    def bankdep(self, bank):
        return self.bank_free.get(bank)

    def bankdone(self, bank, tok):
        self.bank_free[bank] = tok


def ln_row_math(b, sum_row, sq_row, scr_row, inv, deps, scr_key, eps=EPS):
    t = b.emit("vector", lambda e: e.tensor_scalar(sum_row[0:1, :], sum_row[0:1, :], inv, None, ALU.mult), deps=list(deps) + [b.f(scr_key)], inc="V")
    t = b.emit("vector", lambda e: e.tensor_scalar(sq_row[0:1, :], sq_row[0:1, :], inv, None, ALU.mult), deps=[t], inc="V")
    t = b.emit("vector", lambda e: e.tensor_tensor(scr_row[0:1, :], sum_row[0:1, :], sum_row[0:1, :], ALU.mult), deps=[t], inc="V")
    t = b.emit("vector", lambda e: e.tensor_tensor(sq_row[0:1, :], sq_row[0:1, :], scr_row[0:1, :], ALU.subtract), deps=[t], inc="V")
    epc, ept = b.cval(eps)
    nhc, nht = b.cval(-0.5)
    t = b.emit("scalar", lambda e: e.activation(scr_row[0:1, :], sq_row[0:1, :], AF.Ln, bias=epc[0:1, :]), deps=[t, ept], inc="S")
    t = b.emit("scalar", lambda e: e.activation(scr_row[0:1, :], scr_row[0:1, :], AF.Exp, scale=nhc[0:1, :]), deps=[t, nht], inc="S")
    t = b.emit("vector", lambda e: e.scalar_tensor_tensor(sq_row[0:1, :], sum_row[0:1, :], -1.0, scr_row[0:1, :], ALU.mult, ALU.mult), deps=[t], inc="V")
    t = b.emit("vector", lambda e: e.tensor_copy(sum_row[0:1, :], scr_row[0:1, :]), deps=[t], inc="V")
    b.setf(scr_key, t)
    return t


def ln_stats(b, get_chunk, nchunks, ndim, sumb, sqb, a_row, b_row, scr_row,
             rows_war, scr_key):
    done = []
    for ft in range(NF):
        fs = slice(ft * FT, (ft + 1) * FT)
        mm = None
        for c in range(nchunks):
            xap, xd = get_chunk(c)
            mm = b.emit("tensor", lambda e, o=b.psb[sumb][0:1, :], r=xap, fs=fs,
                        st=(c == 0), sp=(c == nchunks - 1):
                        e.matmul(o, b.ones_col[:, 0:1], r[:, fs], start=st, stop=sp),
                        deps=[xd, b.bankdep(sumb)], inc="T" if c == nchunks - 1 else None)
        e1 = b.emit("vector", lambda e, o=a_row, fs=fs: e.tensor_copy(o[0:1, fs], b.psb[sumb][0:1, 0:FT]),
                    deps=[mm, rows_war], inc="V")
        b.bankdone(sumb, e1)
        done.append(e1)
        mm = None
        for c in range(nchunks):
            xap, xd = get_chunk(c)
            sqt = b.emit("scalar", lambda e, i=xap, fs=fs: e.activation(b.sq[:, 0:FT], i[:, fs], AF.Square),
                         deps=[xd, b.f("sq")], inc="S")
            mm = b.emit("tensor", lambda e, o=b.psb[sqb][0:1, :],
                        st=(c == 0), sp=(c == nchunks - 1):
                        e.matmul(o, b.ones_col[:, 0:1], b.sq[:, 0:FT], start=st, stop=sp),
                        deps=[sqt, b.bankdep(sqb)], inc="T")
            b.setf("sq", mm)
        e2 = b.emit("vector", lambda e, o=b_row, fs=fs: e.tensor_copy(o[0:1, fs], b.psb[sqb][0:1, 0:FT]),
                    deps=[mm, rows_war], inc="V")
        b.bankdone(sqb, e2)
        done.append(e2)
    return ln_row_math(b, a_row, b_row, scr_row, 1.0 / (128 * ndim), done, scr_key)


def ln_apply(b, out_ap, x_ap, x_dep, a_row, b_row, rows_dep, bankA, bankB,
             out_war=None):
    last = None
    for ft in range(NF):
        fs = slice(ft * FT, (ft + 1) * FT)
        ta = b.emit("tensor", lambda e, r=a_row, fs=fs:
                    e.matmul(b.psb[bankA][:, :], b.ones_r128[0:1, :], r[0:1, fs], start=True, stop=True),
                    deps=[rows_dep, b.bankdep(bankA)], inc="T")
        tb = b.emit("tensor", lambda e, r=b_row, fs=fs:
                    e.matmul(b.psb[bankB][:, :], b.ones_r128[0:1, :], r[0:1, fs], start=True, stop=True),
                    deps=[b.bankdep(bankB)], inc="T")
        t1 = b.emit("vector", lambda e, x=x_ap, fs=fs:
                    e.tensor_tensor(b.lnt[:, 0:FT], x[:, fs], b.psb[bankA][:, :], ALU.mult),
                    deps=[x_dep, ta, b.f("lnt")], inc="V")
        b.bankdone(bankA, t1)
        t2 = b.emit("vector", lambda e, o=out_ap, fs=fs:
                    e.tensor_tensor(o[:, fs], b.lnt[:, 0:FT], b.psb[bankB][:, :], ALU.add),
                    deps=[tb, t1, out_war], inc="V")
        b.bankdone(bankB, t2)
        b.setf("lnt", t2)
        last = t2
    return last


def bcast_col(b, dst_col, src11, dep, bank):
    mm = b.emit("tensor", lambda e: e.matmul(b.psb[bank][:, 0:1], b.ones_r128[0:1, :], src11[0:1, 0:1], start=True, stop=True),
                deps=[dep, b.bankdep(bank)], inc="T")
    ev = b.emit("vector", lambda e: e.tensor_copy(dst_col[:, 0:1], b.psb[bank][:, 0:1]), deps=[mm], inc="V")
    b.bankdone(bank, ev)
    return ev


def part_max_to_scalar(b, col_ap, col_dep, identS, id_dep, out11, bank):
    tp = b.emit("tensor", lambda e: e.transpose(b.psb[bank][0:1, 0:128], col_ap[:, 0:1], identS[:, :]),
                deps=[col_dep, id_dep, b.bankdep(bank)], inc="T")
    mv = b.emit("vector", lambda e: e.tensor_copy(b.mrow[0:1, :], b.psb[bank][0:1, 0:128]),
                deps=[tp, b.f("mrow")], inc="V")
    b.bankdone(bank, mv)
    mv = b.emit("vector", lambda e: e.reduce_max(out11[0:1, 0:1], b.mrow[0:1, :], AX.X), deps=[mv], inc="V")
    b.setf("mrow", mv)
    return mv


def make_projection(b, WTd, wbuf, nk):
    def load_wtiles(wname, m, buf_i, war_dep):
        toks = []
        for kc in range(nk):
            src = WTd[wname][kc * 128:(kc + 1) * 128, m * 128:(m + 1) * 128]
            toks.append(b.dma("gpsimd", wbuf[buf_i][kc][:, :], src,
                              deps=[war_dep, b.f(("wb", buf_i, kc))], sem=f"WD{buf_i}"))
        return toks

    def projection(wname, nm, rhs, rhs_deps, evac_fn, prev_w_consumer, banks=(0, 1)):
        out_toks = [None] * nm
        last_mm = None
        wtoks = load_wtiles(wname, 0, 0, prev_w_consumer)
        for m in range(nm):
            bi = m % 2
            nwtoks = load_wtiles(wname, m + 1, 1 - bi, prev_w_consumer) if m + 1 < nm else None
            for ft in range(NF):
                bank = banks[ft % 2]
                mm = None
                for kc in range(nk):
                    mm = b.emit("tensor",
                                lambda e, o=b.psb[bank][:, :], l=wbuf[bi][kc], r=rhs[kc],
                                fs=slice(ft * FT, (ft + 1) * FT), st=(kc == 0), sp=(kc == nk - 1):
                                e.matmul(o, l[:, :], r[:, fs], start=st, stop=sp),
                                deps=[wtoks[kc], rhs_deps[kc], b.bankdep(bank)],
                                inc="T" if kc == nk - 1 else None)
                last_mm = mm
                cons = evac_fn(m, ft, bank, mm)
                b.bankdone(bank, cons)
                out_toks[m] = cons
            for kc in range(nk):
                b.setf(("wb", bi, kc), last_mm)
            wtoks = nwtoks
        return out_toks, last_mm
    return projection


def qdq3(b, out_ap, in_ap, sicol, scol, col_deps, deps, out_scr, scr_key,
         lo=-32768.0, hi=QMAX):
    t1 = b.emit("vector", lambda e: e.tensor_scalar(out_scr, in_ap, sicol, RC, ALU.mult, ALU.add),
                deps=list(deps) + list(col_deps) + [b.f(scr_key)], inc="V")
    t2 = b.emit("vector", lambda e: e.tensor_scalar(out_scr, out_scr, RC, lo, ALU.subtract, ALU.max),
                deps=[t1], inc="V")
    t3 = b.emit("vector", lambda e: e.tensor_scalar(out_ap, out_scr, hi, scol, ALU.min, ALU.mult),
                deps=[t2], inc="V")
    b.setf(scr_key, t3)
    return t3



def build_launch1():
    b = PB(arena_kb=196)
    X = b.din("x", [D, T])
    WT = {w: b.din(w, [D, D]) for w in ("wkT", "wvT", "wrT", "woT")}
    CIN = {n: b.din(n, [D, 1]) for n in ("tmk", "tmv", "tmr", "tdec", "tfirst", "boc")}
    IDD = b.din("ident", [128, 128])
    R1o = b.dout("R1", [D, T])
    XLN1o = b.dout("XLN1", [D, T])
    M1o = b.dout("M1", [1, 1])

    off = 0
    def fx(name, shape, dt):
        nonlocal off
        h = b.place(name, shape, dt, off)
        per = int(np.prod(shape[1:])) * mybir.dt.size(dt)
        off = (off + per + 31) // 32 * 32
        return h
    a1 = fx("a1", [1, T], F32)
    b1 = fx("b1", [1, T], F32)
    b.sq = fx("sq", [128, FT], F32)
    b.lnt = fx("lnt", [128, FT], F32)
    b.ones_col = fx("ones_col", [128, 1], F32)
    b.ones_r128 = fx("ones_r128", [1, 128], F32)
    cols = {n: fx(n + "_s", [128, DC], F32) for n in ("tmk", "tmv", "tmr", "td", "eu", "bo")}
    identS = fx("identS", [128, 128], F32)
    mrow = fx("mrow", [1, 128], F32)
    m1s = fx("m1s", [1, 1], F32)
    mposA = fx("mposA", [128, 1], F32)
    mposB = fx("mposB", [128, 1], F32)
    b.make_cpool(fx)
    assert off <= 28 * KB, off
    FIX = 28 * KB

    kin = [b.place(f"kin{c}", [128, T], BF16, FIX + c * 4 * KB) for c in range(DC)]
    vin = [b.place(f"vin{c}", [128, T], BF16, FIX + c * 4 * KB) for c in range(DC)]
    rin = [b.place(f"rin{c}", [128, T], BF16, FIX + 32 * KB + c * 4 * KB) for c in range(DC)]
    wbuf = [[b.place(f"wb{i}_{kc}", [128, 128], BF16, FIX + 64 * KB + (i * DC + kc) * 256)
             for kc in range(DC)] for i in range(2)]
    ek = [b.place(f"ek{c}", [128, T], F32, FIX + 72 * KB + c * 8 * KB) for c in range(DC)]
    wkv = [b.place(f"wkv{c}", [128, T], F32, FIX + 72 * KB + c * 8 * KB) for c in range(DC)]
    oc = [b.place(f"oc{c}", [128, T], F32, FIX + 72 * KB + c * 8 * KB) for c in range(DC)]
    r0c = b.place("r0c", [128, T], F32, FIX + 136 * KB)
    zc = b.place("zc", [128, T], F32, FIX + 144 * KB)
    ekvc = b.place("ekvc", [128, T], F32, FIX + 136 * KB)
    numc = b.place("numc", [128, T], F32, FIX + 144 * KB)
    denc = b.place("denc", [128, T], F32, FIX + 152 * KB)
    xtra = b.place("xtra", [128, T], F32, FIX + 160 * KB)
    rt1 = b.place("rt1", [1, T], F32, FIX + 160 * KB)
    rwkv = [b.place(f"rwkv{c}", [128, T], BF16, FIX + 136 * KB + c * 4 * KB) for c in range(DC)]
    xlnc = b.place("xlnc", [128, T], F32, FIX + 152 * KB)
    a2 = b.place("a2", [1, T], F32, FIX + 0)
    b2 = b.place("b2", [1, T], F32, FIX + 8 * KB)
    rt2 = b.place("rt2", [1, T], F32, FIX + 16 * KB)

    b.psb = [b.psum(f"ps{i}") for i in range(8)]

    i0 = b.emit("vector", lambda e: e.memset(b.ones_col[:, :], 1.0), inc="V")
    i1 = b.emit("vector", lambda e: e.memset(b.ones_r128[0:1, :], 1.0), inc="V")
    ld_id = b.dma("sync", identS[:, :], IDD)
    coltok = {}
    for c in range(DC):
        rs = slice(c * 128, (c + 1) * 128)
        for src, dst in (("tmk", "tmk"), ("tmv", "tmv"), ("tmr", "tmr"), ("boc", "bo")):
            coltok[(dst, c)] = b.dma("sync", cols[dst][:, c:c + 1], CIN[src][rs, :])
        t_d = b.dma("sync", cols["td"][:, c:c + 1], CIN["tdec"][rs, :])
        t_f = b.dma("sync", cols["eu"][:, c:c + 1], CIN["tfirst"][rs, :])
        t = b.emit("scalar", lambda e, c=c: e.activation(cols["td"][:, c:c + 1], cols["td"][:, c:c + 1], AF.Exp),
                   deps=[t_d], inc="S")
        m1c, m1t = b.cval(-1.0)
        coltok[("td", c)] = b.emit("scalar", lambda e, c=c, m1c=m1c: e.activation(cols["td"][:, c:c + 1], cols["td"][:, c:c + 1], AF.Exp, scale=m1c[:, :]),
                                   deps=[t, m1t], inc="S")
        coltok[("eu", c)] = b.emit("scalar", lambda e, c=c: e.activation(cols["eu"][:, c:c + 1], cols["eu"][:, c:c + 1], AF.Exp),
                                   deps=[t_f], inc="S")

    s_done = []
    for c in range(DC):
        ld = b.dma("sync", r0c[:, :], X[c * 128:(c + 1) * 128, :], deps=[b.f("r0c"), i0])
        cons = []
        for ft in range(NF):
            fs = slice(ft * FT, (ft + 1) * FT)
            mm = b.emit("tensor", lambda e, o=b.psb[ft][0:1, :], fs=fs,
                        st=(c == 0), sp=(c == DC - 1):
                        e.matmul(o, b.ones_col[:, 0:1], r0c[:, fs], start=st, stop=sp),
                        deps=[ld, b.bankdep(ft)], inc="T")
            cons.append(mm)
            sqt = b.emit("scalar", lambda e, fs=fs: e.activation(b.sq[:, 0:FT], r0c[:, fs], AF.Square),
                         deps=[ld, b.f("sq")], inc="S")
            mm2 = b.emit("tensor", lambda e, o=b.psb[4 + ft][0:1, :],
                         st=(c == 0), sp=(c == DC - 1):
                         e.matmul(o, b.ones_col[:, 0:1], b.sq[:, 0:FT], start=st, stop=sp),
                         deps=[sqt, b.bankdep(4 + ft)], inc="T")
            b.setf("sq", mm2)
            cons.append(mm2)
        b.setf("r0c", cons[-1])
        if c == DC - 1:
            s_done = cons
    ev = []
    for ft in range(NF):
        fs = slice(ft * FT, (ft + 1) * FT)
        e1 = b.emit("vector", lambda e, fs=fs, i=b.psb[ft]: e.tensor_copy(a1[0:1, fs], i[0:1, 0:FT]),
                    deps=s_done, inc="V")
        b.bankdone(ft, e1)
        e2 = b.emit("vector", lambda e, fs=fs, i=b.psb[4 + ft]: e.tensor_copy(b1[0:1, fs], i[0:1, 0:FT]),
                    deps=s_done, inc="V")
        b.bankdone(4 + ft, e2)
        ev += [e1, e2]
    stats1 = ln_row_math(b, a1, b1, rt1, 1.0 / D, ev, "rt1")
    b.setf("xtra", stats1)

    def build_mix(out_bf, col_ap, col_dep, zdep, war):
        t1 = b.emit("vector", lambda e: e.tensor_tensor(xtra[:, 1:T], zc[:, 1:T], zc[:, 0:T - 1], ALU.subtract),
                    deps=[zdep, b.f("xtra")], inc="V")
        t1b = b.emit("vector", lambda e: e.tensor_copy(xtra[:, 0:1], zc[:, 0:1]), deps=[t1], inc="V")
        t2 = b.emit("vector", lambda e, o=out_bf: e.scalar_tensor_tensor(o[:, 1:T], xtra[:, 1:T], col_ap, zc[:, 0:T - 1], ALU.mult, ALU.add),
                    deps=[t1b, col_dep, war], inc="V")
        t3 = b.emit("vector", lambda e, o=out_bf: e.tensor_scalar(o[:, 0:1], xtra[:, 0:1], col_ap, None, ALU.mult),
                    deps=[t2], inc="V")
        b.setf("xtra", t3)
        return t3

    def build_mixes(spec, war):
        toks = {name: [None] * DC for _, name in spec}
        for c in range(DC):
            ld = b.dma("sync", r0c[:, :], X[c * 128:(c + 1) * 128, :], deps=[b.f("r0c"), stats1])
            zt = ln_apply(b, zc, r0c, ld, a1, b1, stats1, 6, 7, out_war=b.f("zc"))
            b.setf("zc", zt)
            last = zt
            for bufs, name in spec:
                last = build_mix(bufs[c], cols[name][:, c:c + 1], coltok[(name, c)], zt, war)
                toks[name][c] = last
            b.setf("r0c", last)
            b.setf("zc", last)
        return toks

    mixk = build_mixes([(kin, "tmk")], None)["tmk"]

    def load_wtiles(wname, m, buf_i, war_dep):
        toks = []
        for kc in range(DC):
            src = WT[wname][kc * 128:(kc + 1) * 128, m * 128:(m + 1) * 128]
            toks.append(b.dma("gpsimd", wbuf[buf_i][kc][:, :], src,
                              deps=[war_dep, b.f(("wb", buf_i, kc))], sem="WD"))
        return toks

    def projection(wname, rhs, rhs_deps, evac_fn, prev_w_consumer):
        out_toks = [None] * DC
        last_mm = None
        wtoks = load_wtiles(wname, 0, 0, prev_w_consumer)
        for m in range(DC):
            bi = m % 2
            nwtoks = load_wtiles(wname, m + 1, 1 - bi, prev_w_consumer) if m + 1 < DC else None
            for ft in range(NF):
                bank = ft % 2
                mm = None
                for kc in range(DC):
                    mm = b.emit("tensor",
                                lambda e, o=b.psb[bank][:, :], l=wbuf[bi][kc], r=rhs[kc],
                                fs=slice(ft * FT, (ft + 1) * FT), st=(kc == 0), sp=(kc == DC - 1):
                                e.matmul(o, l[:, :], r[:, fs], start=st, stop=sp),
                                deps=[wtoks[kc], rhs_deps[kc], b.bankdep(bank)],
                                inc="T" if kc == DC - 1 else None)
                last_mm = mm
                cons = evac_fn(m, ft, bank, mm)
                b.bankdone(bank, cons)
                out_toks[m] = cons
            for kc in range(DC):
                b.setf(("wb", bi, kc), last_mm)
            wtoks = nwtoks
        return out_toks, last_mm

    mix_dep_k = [mixk[-1]] * DC
    def evac_ek(m, ft, bank, mm):
        fs = slice(ft * FT, (ft + 1) * FT)
        return b.emit("scalar", lambda e, o=ek[m], fs=fs, i=b.psb[bank]:
                      e.activation(o[:, fs], i[:, :], AF.Exp), deps=[mm], inc="S")
    ek_toks, k_lastmm = projection("wkT", kin, mix_dep_k, evac_ek, None)

    mt = build_mixes([(vin, "tmv"), (rin, "tmr")], k_lastmm)
    mixv, mixr = mt["tmv"], mt["tmr"]
    mix_dep_v = [mixv[-1]] * DC
    mix_dep_r = [mixr[-1]] * DC
    b.setf("ekvc", mixr[-1])
    b.setf("numc", mixr[-1])
    b.setf("denc", mixr[-1])

    wkv_toks = [None] * DC
    def evac_ekv(m, ft, bank, mm):
        fs = slice(ft * FT, (ft + 1) * FT)
        t = b.emit("vector", lambda e, fs=fs, x=ek[m], p=b.psb[bank]:
                   e.tensor_tensor(ekvc[:, fs], x[:, fs], p[:, :], ALU.mult),
                   deps=[mm, ek_toks[m], b.f("ekvc")], inc="V")
        if ft < NF - 1:
            return t
        z0 = b.emit("vector", lambda e: e.memset(numc[:, 0:1], 0.0), deps=[b.f("numc")], inc="V")
        z1 = b.emit("vector", lambda e: e.memset(denc[:, 0:1], 0.0), deps=[b.f("denc")], inc="V")
        tdb = cols["td"][:, m:m + 1].broadcast_to([128, T - 1])
        s1 = b.emit("vector", lambda e: e.tensor_tensor_scan(numc[:, 1:T], tdb, ekvc[:, 0:T - 1], 0.0, ALU.mult, ALU.add),
                    deps=[z0, t, coltok[("td", m)]], inc="V")
        s2 = b.emit("vector", lambda e, x=ek[m]: e.tensor_tensor_scan(denc[:, 1:T], tdb, x[:, 0:T - 1], 0.0, ALU.mult, ALU.add),
                    deps=[z1], inc="V")
        eucol = cols["eu"][:, m:m + 1]
        n1 = b.emit("vector", lambda e: e.scalar_tensor_tensor(xtra[:, :], ekvc[:, :], eucol, numc[:, :], ALU.mult, ALU.add),
                    deps=[s1, coltok[("eu", m)], b.f("xtra")], inc="V")
        n2 = b.emit("vector", lambda e, x=ek[m]: e.scalar_tensor_tensor(ekvc[:, :], x[:, :], eucol, denc[:, :], ALU.mult, ALU.add),
                    deps=[s2, n1], inc="V")
        b.setf("denc", n2)
        b.setf("numc", n1)
        rcp = b.emit("vector", lambda e: e.reciprocal(numc[:, :], ekvc[:, :]), deps=[n2, n1], inc="V")
        b.setf("ekvc", rcp)
        wt = b.emit("vector", lambda e, o=wkv[m]: e.tensor_tensor(o[:, :], xtra[:, :], numc[:, :], ALU.mult),
                    deps=[rcp], inc="V")
        b.setf("numc", wt)
        b.setf("xtra", wt)
        wkv_toks[m] = wt
        return wt
    _, v_lastmm = projection("wvT", vin, mix_dep_v, evac_ekv, k_lastmm)

    def evac_r(m, ft, bank, mm):
        fs = slice(ft * FT, (ft + 1) * FT)
        sg = b.emit("scalar", lambda e, i=b.psb[bank]: e.activation(b.sq[:, 0:FT], i[:, :], AF.Sigmoid),
                    deps=[mm, b.f("sq")], inc="S")
        rw = b.emit("vector", lambda e, o=rwkv[m], fs=fs, w=wkv[m]:
                    e.tensor_tensor(o[:, fs], b.sq[:, 0:FT], w[:, fs], ALU.mult),
                    deps=[sg, wkv_toks[m], wkv_toks[DC - 1]], inc="V")
        b.setf("sq", rw)
        return rw
    rw_toks, r_lastmm = projection("wrT", rin, mix_dep_r, evac_r, v_lastmm)

    all_rw = rw_toks[DC - 1]
    def evac_o(m, ft, bank, mm):
        fs = slice(ft * FT, (ft + 1) * FT)
        return b.emit("vector", lambda e, o=oc[m], fs=fs, i=b.psb[bank], s=cols["bo"][:, m:m + 1]:
                      e.tensor_scalar(o[:, fs], i[:, :], s, None, ALU.add),
                      deps=[mm, coltok[("bo", m)], all_rw], inc="V")
    o_toks, o_lastmm = projection("woT", rwkv, rw_toks, evac_o, r_lastmm)

    def get_o(c):
        return oc[c], o_toks[c]
    stats2 = ln_stats(b, get_o, DC, DC, 4, 5, a2, b2, rt2, v_lastmm, "rt2")

    b.setf("r0c", o_lastmm)
    b.setf("zc", o_lastmm)
    b.setf("xlnc", o_lastmm)
    r1_toks = [None] * DC
    for c in range(DC):
        ld = b.dma("sync", r0c[:, :], X[c * 128:(c + 1) * 128, :], deps=[b.f("r0c"), stats2])
        lt = ln_apply(b, zc, oc[c], o_toks[c], a2, b2, stats2, 6, 7, out_war=b.f("zc"))
        ad = b.emit("vector", lambda e, o=oc[c]: e.tensor_tensor(o[:, :], r0c[:, :], zc[:, :], ALU.add),
                    deps=[lt, ld], inc="V")
        b.setf("r0c", ad)
        b.setf("zc", ad)
        b.dma("gpsimd", R1o[c * 128:(c + 1) * 128, :], oc[c][:, :], deps=[ad], sem="ST")
        r1_toks[c] = ad

    def get_r1(c):
        return oc[c], r1_toks[c]
    stats3 = ln_stats(b, get_r1, DC, DC, 4, 5, a2, b2, rt2, r1_toks[DC - 1], "rt2")
    mz = b.emit("vector", lambda e: e.memset(mposA[:, :], 0.0), inc="V")
    mx = mz
    for c in range(DC):
        lt = ln_apply(b, xlnc, oc[c], r1_toks[c], a2, b2, stats3, 6, 7, out_war=b.f("xlnc"))
        stx = b.dma("gpsimd", XLN1o[c * 128:(c + 1) * 128, :], xlnc[:, :], deps=[lt], sem="ST")
        rm = b.emit("vector", lambda e: e.reduce_max(zc[:, 0:1], xlnc[:, :], AX.X, apply_absolute_value=True),
                    deps=[lt, mx], inc="V")
        src, dst = (mposA, mposB) if c % 2 == 0 else (mposB, mposA)
        mx = b.emit("vector", lambda e, s=src, d=dst: e.tensor_tensor(d[:, 0:1], s[:, 0:1], zc[:, 0:1], ALU.max),
                    deps=[rm, mx], inc="V")
        b.setf("xlnc", stx)
    final_m = mposA if DC % 2 == 0 else mposB
    tp = b.emit("tensor", lambda e: e.transpose(b.psb[3][0:1, 0:128], final_m[:, 0:1], identS[:, :]),
                deps=[mx, ld_id, b.bankdep(3)], inc="T")
    mv = b.emit("vector", lambda e: e.tensor_copy(mrow[0:1, :], b.psb[3][0:1, 0:128]), deps=[tp], inc="V")
    b.bankdone(3, mv)
    mv = b.emit("vector", lambda e: e.reduce_max(m1s[0:1, 0:1], mrow[0:1, :], AX.X), deps=[mv], inc="V")
    b.dma("gpsimd", M1o, m1s[0:1, 0:1], deps=[mv], sem="ST")

    b.emit("gpsimd", lambda e: e.nop(), deps=[("ST", b.semn["ST"])])
    return b





def build_launch2():
    b = PB(arena_kb=204)
    XLN = b.din("xln", [D, T])
    M1A = b.din("m1a", [1, NCORES])
    WTd = {"wupT": b.din("wupT", [D, 3 * D]), "wgT": b.din("wgT", [D, 3 * D])}
    WN = {"wupT": b.din("wupN", [3 * D, D]), "wgT": b.din("wgN", [3 * D, D])}
    BMP = b.din("bmap", [32, D])
    IDD = b.din("ident", [128, 128])
    UPo = b.dout("UP", [3 * D, T])
    GAo = b.dout("GATE", [3 * D, T])
    M2o = b.dout("M2", [1, 1])
    M3o = b.dout("M3", [1, 1])

    off = 0
    def fx(name, shape, dt):
        nonlocal off
        h = b.place(name, shape, dt, off)
        off = (off + int(np.prod(shape[1:])) * mybir.dt.size(dt) + 31) // 32 * 32
        return h
    b.ones_col = fx("ones_col", [128, 1], F32)
    b.ones_r128 = fx("ones_r128", [1, 128], F32)
    b.mrow = fx("mrow", [1, 128], F32)
    b.make_cpool(fx)
    identS = fx("identS", [128, 128], F32)
    bmapS = fx("bmapS", [32, D], F32)
    m8 = fx("m8", [1, NCORES], F32)
    m11 = fx("m11", [1, 4], F32)
    scol = fx("scol", [128, 1], F32)
    sicol = fx("sicol", [128, 1], F32)
    maccA = fx("maccA", [128, 2], F32)
    maccB = fx("maccB", [128, 2], F32)
    mc11 = fx("mc11", [1, 2], F32)
    xq = [fx(f"xq{c}", [128, T], BF16) for c in range(DC)]
    ldb = fx("ldb", [128, T], F32)
    qscr = fx("qscr", [128, T], F32)
    stg = [fx(f"stg{i}", [128, T], F32) for i in range(2)]
    wnat = [fx(f"wnat{i}", [128, D], F32) for i in range(2)]
    wtile = [fx(f"wtile{i}", [128, 128], F32) for i in range(2)]
    dscr = [fx(f"dscr{i}", [128, 128], F32) for i in range(2)]
    sch = fx("sch", [128, 32], F32)
    sTall = fx("sTall", [32, 3 * D], F32)
    iTall = fx("iTall", [32, 3 * D], F32)
    wqA = [fx(f"wqA{kc}", [128, 3 * D], BF16) for kc in range(DC)]
    wqB = [fx(f"wqB{kc}", [128, 3 * D], BF16) for kc in range(DC)]
    b.psb = [b.psum(f"ps{i}") for i in range(8)]

    i0 = b.emit("vector", lambda e: e.memset(b.ones_col[:, :], 1.0), inc="V")
    i1 = b.emit("vector", lambda e: e.memset(b.ones_r128[0:1, :], 1.0), inc="V")
    ld_id = b.dma("sync", identS[:, :], IDD)
    ld_bm = b.dma("sync", bmapS[:, :], BMP)
    ldm = b.dma("sync", m8[0:1, :], M1A)
    t = b.emit("vector", lambda e: e.reduce_max(m11[0:1, 0:1], m8[0:1, :], AX.X), deps=[ldm], inc="V")
    t = b.emit("vector", lambda e: e.tensor_scalar(m11[0:1, 1:2], m11[0:1, 0:1], 1.0 / QMAX, 1e-12, ALU.mult, ALU.max), deps=[t], inc="V")
    t = b.emit("vector", lambda e: e.reciprocal(m11[0:1, 2:3], m11[0:1, 1:2]), deps=[t], inc="V")
    c1 = bcast_col(b, scol, m11[0:1, 1:2], t, 2)
    c2 = bcast_col(b, sicol, m11[0:1, 2:3], t, 2)

    xq_toks = [None] * DC
    for c in range(DC):
        ld = b.dma("sync", ldb[:, :], XLN[c * 128:(c + 1) * 128, :], deps=[b.f("ldb"), i0])
        tq = qdq3(b, xq[c][:, :], ldb[:, :], sicol[:, 0:1], scol[:, 0:1],
                  [c1, c2], [ld], qscr[:, :], "qscr")
        b.setf("ldb", tq)
        xq_toks[c] = tq
    xq_last = [xq_toks[-1]] * DC

    NBLK = D // 32

    def lpbq_dequant(wname, wq, war):
        for oc in range(3 * DC):
            bi = oc % 2
            ldw = b.dma("sync", wnat[bi][:, :], WN[wname][oc * 128:(oc + 1) * 128, :],
                        deps=[b.f(("wnat", bi))])
            rmt = b.emit("vector", lambda e, w=wnat[bi]:
                         e.reduce_max(sch[:, :], w[:, :].rearrange("p (k e) -> p k e", e=32), AX.X,
                                      apply_absolute_value=True),
                         deps=[ldw, b.f("sch")], inc="V")
            b.setf(("wnat", bi), rmt)
            sst = b.emit("vector", lambda e: e.tensor_scalar(sch[:, :], sch[:, :], 1.0 / 7.0, 1e-12, ALU.mult, ALU.max),
                         deps=[rmt], inc="V")
            tp = b.emit("tensor", lambda e: e.transpose(b.psb[4][0:32, 0:128], sch[:, :], identS[:, :]),
                        deps=[sst, ld_id, b.bankdep(4)], inc="T")
            ev = b.emit("vector", lambda e, oc=oc: e.tensor_copy(sTall[0:32, oc * 128:(oc + 1) * 128], b.psb[4][0:32, 0:128]),
                        deps=[tp, war], inc="V")
            b.bankdone(4, ev)
            b.setf("sch", ev)
            b.setf("sTall", ev)
        rcp = b.emit("vector", lambda e: e.reciprocal(iTall[0:32, :], sTall[0:32, :]),
                     deps=[b.f("sTall"), war], inc="V")
        last = None
        for kc in range(DC):
            for ocb in range(3 * DC):
                bi = ocb % 2
                ldw = b.dma("gpsimd", wtile[bi][:, :],
                            WTd[wname][kc * 128:(kc + 1) * 128, ocb * 128:(ocb + 1) * 128],
                            deps=[b.f(("wtile", bi))], sem=f"WD{bi}")
                bsl = slice(ocb * 128, (ocb + 1) * 128)
                mm_s = b.emit("tensor", lambda e, kc=kc, bsl=bsl:
                              e.matmul(b.psb[5][:, 0:128], bmapS[:, kc * 128:(kc + 1) * 128], sTall[0:32, bsl], start=True, stop=True),
                              deps=[ld_bm, rcp, b.bankdep(5)], inc="T")
                mm_i = b.emit("tensor", lambda e, kc=kc, bsl=bsl:
                              e.matmul(b.psb[6][:, 0:128], bmapS[:, kc * 128:(kc + 1) * 128], iTall[0:32, bsl], start=True, stop=True),
                              deps=[rcp, b.bankdep(6)], inc="T")
                d1 = b.emit("vector", lambda e, w=wtile[bi], sc=dscr[bi]:
                            e.tensor_tensor(sc[:, :], w[:, :], b.psb[6][:, 0:128], ALU.mult),
                            deps=[ldw, mm_i, b.f(("dscr", bi))], inc="V")
                b.bankdone(6, d1)
                d2 = b.emit("vector", lambda e, sc=dscr[bi]:
                            e.tensor_scalar(sc[:, :], sc[:, :], RC, RC, ALU.add, ALU.subtract),
                            deps=[d1], inc="V")
                d3 = b.emit("vector", lambda e, sc=dscr[bi]:
                            e.tensor_scalar(sc[:, :], sc[:, :], -8.0, 7.0, ALU.max, ALU.min),
                            deps=[d2], inc="V")
                d4 = b.emit("vector", lambda e, sc=dscr[bi], kc=kc, bsl=bsl:
                            e.tensor_tensor(wq[kc][:, bsl], sc[:, :], b.psb[5][:, 0:128], ALU.mult),
                            deps=[d3, mm_s, b.f(("wq", kc))], inc="V")
                b.bankdone(5, d4)
                b.setf(("dscr", bi), d4)
                b.setf(("wtile", bi), d1)
                b.setf(("wq", kc), d4)
                last = d4
        return last

    mz = b.emit("vector", lambda e: e.memset(maccA[:, :], 0.0), inc="V")
    mstate = {"n": 0, "tok": mz}

    def proj_resident(nm, rhs, rhs_deps, wq, w_dep, evac_fn):
        out_toks = [None] * nm
        for m in range(nm):
            for ft in range(NF):
                bank = ft % 2
                mm = None
                for kc in range(DC):
                    mm = b.emit("tensor",
                                lambda e, o=b.psb[bank][:, :], l=wq[kc], r=rhs[kc], m=m,
                                fs=slice(ft * FT, (ft + 1) * FT), st=(kc == 0), sp=(kc == DC - 1):
                                e.matmul(o, l[:, m * 128:(m + 1) * 128], r[:, fs], start=st, stop=sp),
                                deps=[w_dep, rhs_deps[kc], b.bankdep(bank)],
                                inc="T" if kc == DC - 1 else None)
                cons = evac_fn(m, ft, bank, mm)
                b.bankdone(bank, cons)
                out_toks[m] = cons
        return out_toks, mm

    def make_evac(OUT, mcol_idx):
        def evac(m, ft, bank, mm):
            bi = m % 2
            fs = slice(ft * FT, (ft + 1) * FT)
            ev = b.emit("vector", lambda e, o=stg[bi], fs=fs, p=b.psb[bank]:
                        e.tensor_copy(o[:, fs], p[:, :]),
                        deps=[mm, b.f(("stg", bi))], inc="V")
            n = mstate["n"]
            src, dst = (maccA, maccB) if n % 2 == 0 else (maccB, maccA)
            rm = b.emit("vector", lambda e, o=stg[bi], fs=fs:
                        e.reduce_max(qscr[:, 0:1], o[:, fs], AX.X, apply_absolute_value=True),
                        deps=[ev, b.f("qscr")], inc="V")
            mx = b.emit("vector", lambda e, s=src, d=dst, i=mcol_idx:
                        e.tensor_tensor(d[:, i:i + 1], s[:, i:i + 1], qscr[:, 0:1], ALU.max),
                        deps=[rm, mstate["tok"]], inc="V")
            b.setf("qscr", mx)
            mstate["n"] += 1
            mstate["tok"] = mx
            if ft == NF - 1:
                st = b.dma("gpsimd", OUT[m * 128:(m + 1) * 128, :], stg[bi][:, :], deps=[ev, mx], sem="ST")
                b.setf(("stg", bi), st)
            return mx
        return evac

    dq1 = lpbq_dequant("wupT", wqA, None)
    b.setf("sTall", dq1)
    dq2 = lpbq_dequant("wgT", wqB, dq1)
    up_toks, up_last = proj_resident(3 * DC, xq, xq_last, wqA, dq1, make_evac(UPo, 0))
    _, ga_last = proj_resident(3 * DC, xq, xq_last, wqB, dq2, make_evac(GAo, 1))

    fin = maccA if mstate["n"] % 2 == 0 else maccB
    mv2 = part_max_to_scalar(b, fin[:, 0:1], mstate["tok"], identS, ld_id, mc11[0:1, 0:1], 3)
    mv3 = part_max_to_scalar(b, fin[:, 1:2], mstate["tok"], identS, ld_id, mc11[0:1, 1:2], 3)
    b.dma("gpsimd", M2o, mc11[0:1, 0:1], deps=[mv2], sem="ST")
    b.dma("gpsimd", M3o, mc11[0:1, 1:2], deps=[mv3], sem="ST")
    b.emit("gpsimd", lambda e: e.nop(), deps=[("ST", b.semn["ST"])])
    return b



def build_launch3():
    b = PB(arena_kb=120)
    GA = b.din("gate", [3 * D, T])
    M3A = b.din("m3a", [1, NCORES])
    IDD = b.din("ident", [128, 128])
    GSo = b.dout("GS", [3 * D, T])
    M4o = b.dout("M4", [1, 1])

    off = 0
    def fx(name, shape, dt):
        nonlocal off
        h = b.place(name, shape, dt, off)
        off = (off + int(np.prod(shape[1:])) * mybir.dt.size(dt) + 31) // 32 * 32
        return h
    b.ones_col = fx("ones_col", [128, 1], F32)
    b.ones_r128 = fx("ones_r128", [1, 128], F32)
    b.mrow = fx("mrow", [1, 128], F32)
    b.make_cpool(fx)
    identS = fx("identS", [128, 128], F32)
    m8 = fx("m8", [1, NCORES], F32)
    m11 = fx("m11", [1, 4], F32)
    scol = fx("scol", [128, 1], F32)
    sicol = fx("sicol", [128, 1], F32)
    maccA = fx("maccA", [128, 1], F32)
    maccB = fx("maccB", [128, 1], F32)
    mc11 = fx("mc11", [1, 1], F32)
    ldb = [fx(f"ldb{i}", [128, T], F32) for i in range(2)]
    gq = fx("gq", [128, T], F32)
    sg = fx("sg", [128, T], F32)
    qscr = fx("qscr", [128, T], F32)
    ob = [fx(f"ob{i}", [128, T], F32) for i in range(2)]
    rmx = fx("rmx", [128, 1], F32)
    b.psb = [b.psum(f"ps{i}") for i in range(4)]

    i0 = b.emit("vector", lambda e: e.memset(b.ones_col[:, :], 1.0), inc="V")
    i1 = b.emit("vector", lambda e: e.memset(b.ones_r128[0:1, :], 1.0), inc="V")
    ld_id = b.dma("sync", identS[:, :], IDD)
    ldm = b.dma("sync", m8[0:1, :], M3A)
    t = b.emit("vector", lambda e: e.reduce_max(m11[0:1, 0:1], m8[0:1, :], AX.X), deps=[ldm], inc="V")
    t = b.emit("vector", lambda e: e.tensor_scalar(m11[0:1, 1:2], m11[0:1, 0:1], 1.0 / QMAX, 1e-12, ALU.mult, ALU.max), deps=[t], inc="V")
    t = b.emit("vector", lambda e: e.reciprocal(m11[0:1, 2:3], m11[0:1, 1:2]), deps=[t], inc="V")
    c1 = bcast_col(b, scol, m11[0:1, 1:2], t, 2)
    c2 = bcast_col(b, sicol, m11[0:1, 2:3], t, 2)

    mz = b.emit("vector", lambda e: e.memset(maccA[:, :], 0.0), inc="V")
    mx = mz
    NCH = 3 * DC
    for c in range(NCH):
        bi = c % 2
        ld = b.dma("sync", ldb[bi][:, :], GA[c * 128:(c + 1) * 128, :], deps=[b.f(("ldb", bi)), i0])
        t1 = qdq3(b, gq[:, :], ldb[bi][:, :], sicol[:, 0:1], scol[:, 0:1], [c1, c2],
                  [ld, b.f("gq")], qscr[:, :], "qscr")
        b.setf(("ldb", bi), t1)
        sgt = b.emit("scalar", lambda e: e.activation(sg[:, :], gq[:, :], AF.Sigmoid),
                     deps=[t1, b.f("sg")], inc="S")
        f1 = b.emit("vector", lambda e: e.tensor_scalar(qscr[:, :], sg[:, :], 65536.0, RC, ALU.mult, ALU.add),
                    deps=[sgt, b.f("qscr")], inc="V")
        f2 = b.emit("vector", lambda e: e.tensor_scalar(qscr[:, :], qscr[:, :], RC, 0.0, ALU.subtract, ALU.max),
                    deps=[f1], inc="V")
        f3 = b.emit("vector", lambda e: e.tensor_scalar(sg[:, :], qscr[:, :], 65535.0, 2.0 ** -16, ALU.min, ALU.mult),
                    deps=[f2], inc="V")
        b.setf("qscr", f3)
        b.setf("sg", f3)
        gs = b.emit("vector", lambda e, o=ob[bi]: e.tensor_tensor(o[:, :], gq[:, :], sg[:, :], ALU.mult),
                    deps=[f3, b.f(("ob", bi))], inc="V")
        b.setf("gq", gs)
        b.setf("sg", gs)
        rm = b.emit("vector", lambda e, o=ob[bi]: e.reduce_max(rmx[:, 0:1], o[:, :], AX.X, apply_absolute_value=True),
                    deps=[gs, mx], inc="V")
        src, dst = (maccA, maccB) if c % 2 == 0 else (maccB, maccA)
        mx = b.emit("vector", lambda e, s=src, d=dst: e.tensor_tensor(d[:, 0:1], s[:, 0:1], rmx[:, 0:1], ALU.max),
                    deps=[rm, mx], inc="V")
        st = b.dma("gpsimd", GSo[c * 128:(c + 1) * 128, :], ob[bi][:, :], deps=[gs], sem="ST")
        b.setf(("ob", bi), st)
    fin = maccA if NCH % 2 == 0 else maccB
    mv = part_max_to_scalar(b, fin[:, 0:1], mx, identS, ld_id, mc11[0:1, 0:1], 3)
    b.dma("gpsimd", M4o, mc11[0:1, 0:1], deps=[mv], sem="ST")
    b.emit("gpsimd", lambda e: e.nop(), deps=[("ST", b.semn["ST"])])
    return b



def build_launch4():
    b = PB(arena_kb=140)
    GS = b.din("gs", [3 * D, T])
    UP = b.din("up", [3 * D, T])
    M4A = b.din("m4a", [1, NCORES])
    M2A = b.din("m2a", [1, NCORES])
    IDD = b.din("ident", [128, 128])
    OPo = b.dout("OPRE", [3 * D, T])
    M5o = b.dout("M5", [1, 1])

    off = 0
    def fx(name, shape, dt):
        nonlocal off
        h = b.place(name, shape, dt, off)
        off = (off + int(np.prod(shape[1:])) * mybir.dt.size(dt) + 31) // 32 * 32
        return h
    b.ones_col = fx("ones_col", [128, 1], F32)
    b.ones_r128 = fx("ones_r128", [1, 128], F32)
    b.mrow = fx("mrow", [1, 128], F32)
    b.make_cpool(fx)
    identS = fx("identS", [128, 128], F32)
    m8 = fx("m8", [1, NCORES], F32)
    m11 = fx("m11", [1, 4], F32)
    n11 = fx("n11", [1, 4], F32)
    scol4 = fx("scol4", [128, 1], F32)
    sicol4 = fx("sicol4", [128, 1], F32)
    scol2 = fx("scol2", [128, 1], F32)
    sicol2 = fx("sicol2", [128, 1], F32)
    maccA = fx("maccA", [128, 1], F32)
    maccB = fx("maccB", [128, 1], F32)
    mc11 = fx("mc11", [1, 1], F32)
    lda = [fx(f"lda{i}", [128, T], F32) for i in range(2)]
    ldu = [fx(f"ldu{i}", [128, T], F32) for i in range(2)]
    g2 = fx("g2", [128, T], F32)
    uq = fx("uq", [128, T], F32)
    qscr = fx("qscr", [128, T], F32)
    ob = [fx(f"ob{i}", [128, T], F32) for i in range(2)]
    rmx = fx("rmx", [128, 1], F32)
    b.psb = [b.psum(f"ps{i}") for i in range(4)]

    i0 = b.emit("vector", lambda e: e.memset(b.ones_col[:, :], 1.0), inc="V")
    i1 = b.emit("vector", lambda e: e.memset(b.ones_r128[0:1, :], 1.0), inc="V")
    ld_id = b.dma("sync", identS[:, :], IDD)
    toks = []
    for (MA, row, sc, sic) in ((M4A, m11, scol4, sicol4), (M2A, n11, scol2, sicol2)):
        ldm = b.dma("sync", m8[0:1, :], MA, deps=[b.f("m8")])
        t = b.emit("vector", lambda e, r=row: e.reduce_max(r[0:1, 0:1], m8[0:1, :], AX.X), deps=[ldm], inc="V")
        b.setf("m8", t)
        t = b.emit("vector", lambda e, r=row: e.tensor_scalar(r[0:1, 1:2], r[0:1, 0:1], 1.0 / QMAX, 1e-12, ALU.mult, ALU.max), deps=[t], inc="V")
        t = b.emit("vector", lambda e, r=row: e.reciprocal(r[0:1, 2:3], r[0:1, 1:2]), deps=[t], inc="V")
        toks.append(bcast_col(b, sc, row[0:1, 1:2], t, 2))
        toks.append(bcast_col(b, sic, row[0:1, 2:3], t, 2))

    mz = b.emit("vector", lambda e: e.memset(maccA[:, :], 0.0), inc="V")
    mx = mz
    NCH = 3 * DC
    for c in range(NCH):
        bi = c % 2
        ldg = b.dma("sync", lda[bi][:, :], GS[c * 128:(c + 1) * 128, :], deps=[b.f(("lda", bi)), i0])
        ldp = b.dma("sync", ldu[bi][:, :], UP[c * 128:(c + 1) * 128, :], deps=[b.f(("ldu", bi))])
        t1 = qdq3(b, g2[:, :], lda[bi][:, :], sicol4[:, 0:1], scol4[:, 0:1], toks[:2],
                  [ldg, b.f("g2")], qscr[:, :], "qscr")
        b.setf(("lda", bi), t1)
        t2 = qdq3(b, uq[:, :], ldu[bi][:, :], sicol2[:, 0:1], scol2[:, 0:1], toks[2:],
                  [ldp, b.f("uq")], qscr[:, :], "qscr")
        b.setf(("ldu", bi), t2)
        op = b.emit("vector", lambda e, o=ob[bi]: e.tensor_tensor(o[:, :], g2[:, :], uq[:, :], ALU.mult),
                    deps=[t1, t2, b.f(("ob", bi))], inc="V")
        b.setf("g2", op)
        b.setf("uq", op)
        rm = b.emit("vector", lambda e, o=ob[bi]: e.reduce_max(rmx[:, 0:1], o[:, :], AX.X, apply_absolute_value=True),
                    deps=[op, mx], inc="V")
        src, dst = (maccA, maccB) if c % 2 == 0 else (maccB, maccA)
        mx = b.emit("vector", lambda e, s=src, d=dst: e.tensor_tensor(d[:, 0:1], s[:, 0:1], rmx[:, 0:1], ALU.max),
                    deps=[rm, mx], inc="V")
        st = b.dma("gpsimd", OPo[c * 128:(c + 1) * 128, :], ob[bi][:, :], deps=[op], sem="ST")
        b.setf(("ob", bi), st)
    fin = maccA if NCH % 2 == 0 else maccB
    mv = part_max_to_scalar(b, fin[:, 0:1], mx, identS, ld_id, mc11[0:1, 0:1], 3)
    b.dma("gpsimd", M5o, mc11[0:1, 0:1], deps=[mv], sem="ST")
    b.emit("gpsimd", lambda e: e.nop(), deps=[("ST", b.semn["ST"])])
    return b





def build_launch5a():
    b = PB(arena_kb=204)
    OP = b.din("opre", [3 * D, T])
    M5A = b.din("m5a", [1, NCORES])
    R1i = b.din("r1", [D, T])
    WDT = b.din("wdT", [3 * D, D])
    WDN = b.din("wdN", [D, 3 * D])
    BMP = b.din("bmap96", [96, 3 * D])
    IDD = b.din("ident", [128, 128])
    R2o = b.dout("R2", [D, T])
    XLN2o = b.dout("XLN2", [D, T])

    off = 0
    def fx(name, shape, dt):
        nonlocal off
        h = b.place(name, shape, dt, off)
        off = (off + int(np.prod(shape[1:])) * mybir.dt.size(dt) + 31) // 32 * 32
        return h
    b.ones_col = fx("ones_col", [128, 1], F32)
    b.ones_r128 = fx("ones_r128", [1, 128], F32)
    b.mrow = fx("mrow", [1, 128], F32)
    b.make_cpool(fx)
    identS = fx("identS", [128, 128], F32)
    bmapS = fx("bmapS", [96, 3 * D], F32)
    m8 = fx("m8", [1, NCORES], F32)
    m11 = fx("m11", [1, 4], F32)
    scol = fx("scol", [128, 1], F32)
    sicol = fx("sicol", [128, 1], F32)
    b.sq = fx("sq", [128, FT], F32)
    b.lnt = fx("lnt", [128, FT], F32)
    HT = T // 2
    ldb = [fx(f"ldb{i}", [128, HT], F32) for i in range(2)]
    qscr = fx("qscr", [128, HT], F32)
    assert off <= 44 * KB, off
    FIXD = 44 * KB
    NK = 3 * DC
    wq = [b.place(f"wq{kc}", [128, D], BF16, FIXD + kc * 2 * KB) for kc in range(NK)]
    a2 = b.place("a2", [1, T], F32, FIXD + 0)
    b2 = b.place("b2", [1, T], F32, FIXD + 8 * KB)
    rt2 = b.place("rt2", [1, T], F32, FIXD + 16 * KB)
    RB = FIXD + 48 * KB
    oq = [b.place(f"oq{kc}", [128, HT], BF16, RB + kc * 2 * KB) for kc in range(NK)]
    wnat = [b.place(f"wnat{i}", [128, 3 * D], F32, RB + i * 12 * KB) for i in range(2)]
    wtile = [b.place(f"wtile{i}", [128, 128], F32, RB + 24 * KB + i * 512) for i in range(2)]
    dscr = [b.place(f"dscr{i}", [128, 128], F32, RB + 25 * KB + i * 512) for i in range(2)]
    sch = b.place("sch", [128, 96], F32, RB + 26 * KB)
    sTall = b.place("sTall", [96, D], F32, RB + 27 * KB)
    iTall = b.place("iTall", [96, D], F32, RB + 31 * KB)
    r0c = b.place("r0c", [128, T], F32, RB + 0)
    zc = b.place("zc", [128, T], F32, RB + 8 * KB)
    xlnc = b.place("xlnc", [128, T], F32, RB + 16 * KB)
    RC_ = FIXD + 96 * KB
    oc = [b.place(f"oc{c}", [128, T], F32, RC_ + c * 8 * KB) for c in range(DC)]
    b.psb = [b.psum(f"ps{i}") for i in range(8)]

    i0 = b.emit("vector", lambda e: e.memset(b.ones_col[:, :], 1.0), inc="V")
    i1 = b.emit("vector", lambda e: e.memset(b.ones_r128[0:1, :], 1.0), inc="V")
    ld_id = b.dma("sync", identS[:, :], IDD)
    ld_bm = b.dma("sync", bmapS[:, :], BMP)
    ldm = b.dma("sync", m8[0:1, :], M5A)
    t = b.emit("vector", lambda e: e.reduce_max(m11[0:1, 0:1], m8[0:1, :], AX.X), deps=[ldm], inc="V")
    t = b.emit("vector", lambda e: e.tensor_scalar(m11[0:1, 1:2], m11[0:1, 0:1], 1.0 / QMAX, 1e-12, ALU.mult, ALU.max), deps=[t], inc="V")
    t = b.emit("vector", lambda e: e.reciprocal(m11[0:1, 2:3], m11[0:1, 1:2]), deps=[t], inc="V")
    c1 = bcast_col(b, scol, m11[0:1, 1:2], t, 2)
    c2 = bcast_col(b, sicol, m11[0:1, 2:3], t, 2)

    NBLK96 = 96
    for ocn in range(DC):
        bi = ocn % 2
        ldw = b.dma("sync", wnat[bi][:, :], WDN[ocn * 128:(ocn + 1) * 128, :], deps=[b.f(("wnat", bi))])
        rmt = b.emit("vector", lambda e, w=wnat[bi]:
                     e.reduce_max(sch[:, :], w[:, :].rearrange("p (k e) -> p k e", e=32), AX.X,
                                  apply_absolute_value=True),
                     deps=[ldw, b.f("sch")], inc="V")
        b.setf(("wnat", bi), rmt)
        sst = b.emit("vector", lambda e: e.tensor_scalar(sch[:, :], sch[:, :], 1.0 / 7.0, 1e-12, ALU.mult, ALU.max),
                     deps=[rmt], inc="V")
        tp = b.emit("tensor", lambda e: e.transpose(b.psb[4][0:96, 0:128], sch[:, :], identS[:, :]),
                    deps=[sst, ld_id, b.bankdep(4)], inc="T")
        ev = b.emit("vector", lambda e, ocn=ocn: e.tensor_copy(sTall[0:96, ocn * 128:(ocn + 1) * 128], b.psb[4][0:96, 0:128]),
                    deps=[tp], inc="V")
        b.bankdone(4, ev)
        b.setf("sch", ev)
        b.setf("sTall", ev)
    rcp = b.emit("vector", lambda e: e.reciprocal(iTall[0:96, :], sTall[0:96, :]), deps=[b.f("sTall")], inc="V")
    dq = None
    for kc in range(NK):
        for ocb in range(DC):
            bi = ocb % 2
            ldw = b.dma("gpsimd", wtile[bi][:, :],
                        WDT[kc * 128:(kc + 1) * 128, ocb * 128:(ocb + 1) * 128],
                        deps=[b.f(("wtile", bi))], sem="WD")
            bsl = slice(ocb * 128, (ocb + 1) * 128)
            mm_s = b.emit("tensor", lambda e, kc=kc, bsl=bsl:
                          e.matmul(b.psb[5][:, 0:128], bmapS[:, kc * 128:(kc + 1) * 128], sTall[0:96, bsl], start=True, stop=True),
                          deps=[ld_bm, rcp, b.bankdep(5)], inc="T")
            mm_i = b.emit("tensor", lambda e, kc=kc, bsl=bsl:
                          e.matmul(b.psb[6][:, 0:128], bmapS[:, kc * 128:(kc + 1) * 128], iTall[0:96, bsl], start=True, stop=True),
                          deps=[rcp, b.bankdep(6)], inc="T")
            d1 = b.emit("vector", lambda e, w=wtile[bi], sc=dscr[bi]:
                        e.tensor_tensor(sc[:, :], w[:, :], b.psb[6][:, 0:128], ALU.mult),
                        deps=[ldw, mm_i, b.f(("dscr", bi))], inc="V")
            b.bankdone(6, d1)
            d2 = b.emit("vector", lambda e, sc=dscr[bi]:
                        e.tensor_scalar(sc[:, :], sc[:, :], RC, RC, ALU.add, ALU.subtract), deps=[d1], inc="V")
            d3 = b.emit("vector", lambda e, sc=dscr[bi]:
                        e.tensor_scalar(sc[:, :], sc[:, :], -8.0, 7.0, ALU.max, ALU.min), deps=[d2], inc="V")
            d4 = b.emit("vector", lambda e, sc=dscr[bi], kc=kc, bsl=bsl:
                        e.tensor_tensor(wq[kc][:, bsl], sc[:, :], b.psb[5][:, 0:128], ALU.mult),
                        deps=[d3, mm_s, b.f(("wq", kc))], inc="V")
            b.bankdone(5, d4)
            b.setf(("dscr", bi), d4)
            b.setf(("wtile", bi), d1)
            b.setf(("wq", kc), d4)
            dq = d4

    o_toks = [None] * DC
    for hf in range(2):
        hofs = hf * HT
        oq_toks = [None] * NK
        for kc in range(NK):
            bi = kc % 2
            ld = b.dma("sync", ldb[bi][:, :], OP[kc * 128:(kc + 1) * 128, hofs:hofs + HT],
                       deps=[b.f(("ldb", bi)), c1, dq])
            tq = qdq3(b, oq[kc][:, :], ldb[bi][:, :], sicol[:, 0:1], scol[:, 0:1],
                      [c1, c2], [ld, b.f(("oq", kc))], qscr[:, :], "qscr")
            b.setf(("ldb", bi), tq)
            b.setf(("oq", kc), tq)
            oq_toks[kc] = tq
        for m in range(DC):
            for ft in range(2):
                bank = ft % 2
                fs = slice(ft * FT, (ft + 1) * FT)
                mm = None
                for kc in range(NK):
                    mm = b.emit("tensor",
                                lambda e, o=b.psb[bank][:, :], l=wq[kc], r=oq[kc], m=m, fs=fs,
                                st=(kc == 0), sp=(kc == NK - 1):
                                e.matmul(o, l[:, m * 128:(m + 1) * 128], r[:, fs], start=st, stop=sp),
                                deps=[dq, oq_toks[kc], b.bankdep(bank)],
                                inc="T" if kc == NK - 1 else None)
                ev = b.emit("vector", lambda e, o=oc[m], p=b.psb[bank], m=m, fs=slice(hofs + ft * FT, hofs + (ft + 1) * FT):
                            e.tensor_copy(o[:, fs], p[:, :]),
                            deps=[mm], inc="V")
                b.bankdone(bank, ev)
                o_toks[m] = ev
        proj_last = o_toks[DC - 1]
        for kc in range(NK):
            b.setf(("oq", kc), proj_last)

    def get_o(c):
        return oc[c], o_toks[c]
    stats2 = ln_stats(b, get_o, DC, DC, 4, 5, a2, b2, rt2, o_toks[DC - 1], "rt2")
    b.setf("r0c", o_toks[DC - 1])
    b.setf("zc", o_toks[DC - 1])
    b.setf("xlnc", o_toks[DC - 1])
    r1_toks = [None] * DC
    for c in range(DC):
        ld = b.dma("sync", r0c[:, :], R1i[c * 128:(c + 1) * 128, :], deps=[b.f("r0c"), stats2])
        lt = ln_apply(b, zc, oc[c], o_toks[c], a2, b2, stats2, 6, 7, out_war=b.f("zc"))
        ad = b.emit("vector", lambda e, o=oc[c]: e.tensor_tensor(o[:, :], r0c[:, :], zc[:, :], ALU.add),
                    deps=[lt, ld], inc="V")
        b.setf("r0c", ad)
        b.setf("zc", ad)
        b.dma("gpsimd", R2o[c * 128:(c + 1) * 128, :], oc[c][:, :], deps=[ad], sem="ST")
        r1_toks[c] = ad

    def get_r2(c):
        return oc[c], r1_toks[c]
    stats3 = ln_stats(b, get_r2, DC, DC, 4, 5, a2, b2, rt2, r1_toks[DC - 1], "rt2")
    for c in range(DC):
        lt = ln_apply(b, xlnc, oc[c], r1_toks[c], a2, b2, stats3, 6, 7, out_war=b.f("xlnc"))
        stx = b.dma("gpsimd", XLN2o[c * 128:(c + 1) * 128, :], xlnc[:, :], deps=[lt], sem="ST")
        b.setf("xlnc", stx)
    b.emit("gpsimd", lambda e: e.nop(), deps=[("ST", b.semn["ST"])])
    return b





def build_launch5b(debug=False):
    b = PB(arena_kb=204)
    XLN = b.din("xln2", [D, T])
    R2i = b.din("r2", [D, T])
    WIN = b.din("winT", [D, 4256])
    WOUT = b.din("woutT", [2048, D])
    CW = b.din("cw", [128, 68])
    CB = b.din("cb", [128, 17])
    DTB = b.din("dtb", [32, 1])
    ALOG = b.din("alog", [32, 1])
    DCOL = b.din("dcol", [2048, 1])
    NW = b.din("nw", [2048, 1])
    TRIU = b.din("triu", [64, 64])
    TRIURT = b.din("triurt", [64, 2048])
    R64D = b.din("r64", [32, 2048])
    IDD = b.din("ident", [128, 128])
    R3o = b.dout("R3", [D, T])
    XLN3o = b.dout("XLN3", [D, T])
    OCD = b.nc.dram_tensor("ocd", [D, T], F32, kind="Internal").ap()
    if debug:
        DXB = b.dout("DXB", [2176, 512])
        DXC = b.dout("DXC", [2176, 512])
        DDT = b.dout("DDT", [32, 512])
        DYT = b.dout("DYT", [64, 2048])
        DXL = b.dout("DXL", [64, 2048])
        DLT = b.dout("DLT", [64, 2048])
        DAC = b.dout("DAC", [64, 32])
        DY0 = b.dout("DY0", [64, 64])
        DEP = b.dout("DEP", [32, 64])
        DXT = b.dout("DXT", [64, 2176])
        DG = b.dout("DG", [2048, 512])
        DST = b.dout("DST", [64, 2048])

    TTL = 512
    NTT = T // TTL
    CH = 64
    NCH_TT = TTL // CH

    off = 0
    def fx(name, shape, dt):
        nonlocal off
        h = b.place(name, shape, dt, off)
        off = (off + int(np.prod(shape[1:])) * mybir.dt.size(dt) + 31) // 32 * 32
        return h
    b.ones_col = fx("ones_col", [128, 1], F32)
    b.ones_r128 = fx("ones_r128", [1, 128], F32)
    b.mrow = fx("mrow", [1, 128], F32)
    b.make_cpool(fx)
    identS = fx("identS", [128, 128], F32)
    triuS = fx("triuS", [64, 64], F32)
    acf = fx("acf", [32, 64], F32)
    nacf = fx("nacf", [32, 64], F32)
    ddc = fx("ddc", [32, 64], F32)
    r64S = fx("r64S", [32, 2048], F32)
    ones3264 = fx("ones3264", [32, 64], F32)
    cwS = fx("cwS", [128, 68], F32)
    cbS = fx("cbS", [128, 17], F32)
    dtbS = fx("dtbS", [32, 1], F32)
    acolS = fx("acolS", [32, 1], F32)
    dcolS = [fx(f"dcolS{f}", [128, 1], F32) for f in range(16)]
    nwS = [fx(f"nwS{f}", [128, 1], F32) for f in range(16)]
    b.ccar = [fx(f"ccar{f}", [128, 4], F32) for f in range(17)]
    b.sq = fx("sq", [128, FT], F32)
    b.lnt = fx("lnt", [128, FT], F32)
    dt_tt = fx("dt_tt", [32, TTL], F32)
    dtA_tt = fx("dtA_tt", [32, TTL], F32)
    dscr1 = fx("dscr1", [32, TTL], F32)
    dscr2 = fx("dscr2", [32, TTL], F32)
    acs = fx("acs", [64, 32], F32)
    eac_tm = fx("eac_tm", [64, 32], F32)
    eplus_f = fx("eplus_f", [32, 64], F32)
    emf = fx("emf", [32, 64], F32)
    ddes = fx("ddes", [32, 64], F32)
    gts = fx("gts", [64, 64], F32)
    cfm0 = fx("cfm0", [64, 64], F32)
    grs = fx("grs", [1, TTL], F32)
    gr2 = fx("gr2", [1, TTL], F32)
    triuRTS = fx("triuRTS", [64, 2048], F32)
    P0 = (off + 1023) // KB * KB
    assert P0 <= 42 * KB, P0
    xb = [b.place(f"xb{c}", [128, T], BF16, P0 + c * 4 * KB) for c in range(DC)]
    S0 = b.place("S0", [64, 2048], F32, P0 + 32 * KB)
    S1 = b.place("S1", [64, 2048], F32, P0 + 40 * KB)
    Sping = [S0, S1]
    P1 = P0 + 48 * KB
    xc = [b.place(f"xc{f}", [128, TTL], F32, P1 + f * 2 * KB) for f in range(17)]
    zt = [b.place(f"zt{f}", [128, TTL], F32, P1 + 34 * KB + f * 2 * KB) for f in range(16)]
    P2 = P1 + 66 * KB
    xtm = b.place("xtm", [64, 2176], F32, P2)
    ytm = b.place("ytm", [64, 2048], F32, P2 + 9 * KB)
    xtl = b.place("xtl", [64, 2048], F32, P2 + 17 * KB)
    stc = b.place("stc", [64, 2048], F32, P2 + 25 * KB)
    ets = b.place("ets", [64, 2048], F32, P2 + 33 * KB)
    lamR = b.place("lamR", [32, 2048], F32, P2 + 33 * KB)
    cvs = [b.place(f"cvs{i}", [128, TTL], F32, P2 + 17 * KB + i * 2 * KB) for i in range(2)]
    ostg = b.place("ostg", [128, TTL], F32, P2 + 41 * KB)
    wbuf = [[b.place(f"wb{i}_{kc}", [128, 128], BF16, P2 + 43 * KB + (i * 16 + kc) * 256)
             for kc in range(16)] for i in range(2)]
    LTs = b.place("LTs", [64, 2048], F32, P2 + 43 * KB)
    gbf = [b.place(f"gbf{f}", [128, TTL], BF16, P2 + f * KB) for f in range(16)]
    r0c = b.place("r0c", [128, T], F32, P1 + 0)
    zc2 = b.place("zc2", [128, T], F32, P1 + 8 * KB)
    xlnc = b.place("xlnc", [128, T], F32, P1 + 16 * KB)
    ocs = [b.place("ocs0", [128, T], F32, P1 + 24 * KB)] * 2
    a2 = b.place("a2", [1, T], F32, P1 + 40 * KB)
    b2 = b.place("b2", [1, T], F32, P1 + 48 * KB)
    rt2 = b.place("rt2", [1, T], F32, P1 + 56 * KB)
    a3 = b.place("a3", [1, T], F32, P2)
    b3 = b.place("b3", [1, T], F32, P2 + 8 * KB)
    b.psb = [b.psum(f"ps{i}") for i in range(8)]

    i0 = b.emit("vector", lambda e: e.memset(b.ones_col[:, :], 1.0), inc="V")
    i1 = b.emit("vector", lambda e: e.memset(b.ones_r128[0:1, :], 1.0), inc="V")
    i2 = b.emit("vector", lambda e: e.memset(ones3264[:, :], 1.0), inc="V")
    iS = b.emit("vector", lambda e: e.memset(S0[:, :], 0.0), inc="V")
    ld_id = b.dma("sync", identS[:, :], IDD)
    ld_tr = b.dma("sync", triuS[:, :], TRIU)
    ld_trt = b.dma("sync", triuRTS[:, :], TRIURT)
    ld_r64 = b.dma("sync", r64S[:, :], R64D)
    ld_cw = b.dma("sync", cwS[:, :], CW)
    ld_cb = b.dma("sync", cbS[:, :], CB)
    ld_dtb = b.dma("sync", dtbS[:, :], DTB)
    ld_al = b.dma("sync", acolS[:, :], ALOG)
    dcol_t = [b.dma("sync", dcolS[f][:, :], DCOL[f * 128:(f + 1) * 128, :]) for f in range(16)]
    nw_t = [b.dma("sync", nwS[f][:, :], NW[f * 128:(f + 1) * 128, :]) for f in range(16)]
    xb_t = [b.dma("gpsimd", xb[c][:, :], XLN[c * 128:(c + 1) * 128, :]) for c in range(DC)]
    m1c, m1t = b.cval(-1.0)
    tA = b.emit("scalar", lambda e: e.activation(acolS[:, :], acolS[:, :], AF.Exp), deps=[ld_al], inc="S")
    tA = b.emit("vector", lambda e: e.tensor_scalar(acolS[:, :], acolS[:, :], -1.0, None, ALU.mult), deps=[tA], inc="V")

    ocd_last = None
    chunk_idx = 0
    for tt in range(NTT):
        t0 = tt * TTL
        def win_tile_loads(m, bi):
            toks = []
            mw = 128 if m < 33 else 32
            for kc in range(DC):
                toks.append(b.dma("gpsimd", wbuf[bi][kc][:, 0:mw],
                                  WIN[kc * 128:(kc + 1) * 128, m * 128:m * 128 + mw],
                                  deps=[b.f(("wb", bi, kc))], sem=f"WD{bi}"))
            return toks
        wtoks = win_tile_loads(0, 0)
        dtr_tok = None
        for m in range(34):
            bi = m % 2
            nwt = win_tile_loads(m + 1, 1 - bi) if m + 1 < 34 else None
            MW = 128 if m < 33 else 32
            mm = None
            for kc in range(DC):
                mm = b.emit("tensor",
                            lambda e, o=b.psb[bi], l=wbuf[bi][kc], r=xb[kc], MW=MW,
                            fs=slice(t0, t0 + TTL), st=(kc == 0), sp=(kc == DC - 1):
                            e.matmul(o[0:MW, :], l[:, 0:MW], r[:, fs], start=st, stop=sp),
                            deps=[wtoks[kc], xb_t[kc], b.bankdep(bi)],
                            inc="T" if kc == DC - 1 else None)
            if m < 16:
                ev = b.emit("vector", lambda e, o=zt[m], p=b.psb[bi]: e.tensor_copy(o[:, :], p[:, :]),
                            deps=[mm, b.f(("zt", m))], inc="V")
                b.setf(("zt", m), ev)
            elif m < 33:
                ev = b.emit("vector", lambda e, o=xc[m - 16], p=b.psb[bi]: e.tensor_copy(o[:, :], p[:, :]),
                            deps=[mm, b.f(("xc", m - 16))], inc="V")
                b.setf(("xc", m - 16), ev)
            else:
                ev = b.emit("vector", lambda e, p=b.psb[bi]: e.tensor_copy(dscr1[:, :], p[0:32, :]),
                            deps=[mm, b.f("dscr1")], inc="V")
                dtr_tok = ev
            b.bankdone(bi, ev)
            for kc in range(DC):
                b.setf(("wb", bi, kc), mm)
            wtoks = nwt
        xc_tok = [b.f(("xc", f)) for f in range(16)] + [b.f(("xc", 16))]
        zt_tok = [b.f(("zt", f)) for f in range(16)]

        if debug and tt == 0:
            for f in range(17):
                xc_tok[f] = b.dma("gpsimd", DXB[f * 128:(f + 1) * 128, :], xc[f][:, :], deps=[xc_tok[f]], sem="ST")

        t1 = b.emit("vector", lambda e: e.tensor_scalar(dscr1[:, :], dscr1[:, :], dtbS[:, 0:1], None, ALU.add),
                    deps=[dtr_tok, ld_dtb], inc="V")
        t2 = b.emit("scalar", lambda e: e.activation(dscr2[:, :], dscr1[:, :], AF.Abs),
                    deps=[t1, b.f("dscr2")], inc="S")
        t3 = b.emit("scalar", lambda e, m1c=m1c: e.activation(dscr2[:, :], dscr2[:, :], AF.Exp, scale=m1c[0:32, :]),
                    deps=[t2, m1t], inc="S")
        t4 = b.emit("scalar", lambda e: e.activation(dscr2[:, :], dscr2[:, :], AF.Ln, bias=1.0),
                    deps=[t3], inc="S")
        t5 = b.emit("vector", lambda e: e.tensor_scalar(dt_tt[:, :], dscr1[:, :], 0.0, None, ALU.max),
                    deps=[t1, b.f("dt_tt")], inc="V")
        t6 = b.emit("vector", lambda e: e.tensor_tensor(dt_tt[:, :], dt_tt[:, :], dscr2[:, :], ALU.add),
                    deps=[t4, t5], inc="V")
        b.setf("dscr1", t6)
        b.setf("dscr2", t6)
        dt_tok = b.emit("vector", lambda e: e.tensor_scalar(dtA_tt[:, :], dt_tt[:, :], acolS[:, 0:1], None, ALU.mult),
                        deps=[t6, tA, b.f("dtA_tt")], inc="V")
        b.setf("dt_tt", dt_tok)
        b.setf("dtA_tt", dt_tok)

        for f in range(17):
            wc = [cwS[:, 4 * f + dd:4 * f + dd + 1] for dd in range(4)]
            src = xc[f]
            a_, b_ = cvs[0], cvs[1]
            tconv = b.emit("vector", lambda e, src=src, w3=wc[3], a_=a_:
                           e.tensor_scalar(a_[:, :], src[:, :], w3, None, ALU.mult),
                           deps=[xc_tok[f], ld_cw, b.f("cvs0"), b.f("xtl")], inc="V")
            for dd in (2, 1, 0):
                sh = 3 - dd
                tmain = b.emit("vector", lambda e, src=src, w=wc[dd], sh=sh, a_=a_, b_=b_:
                               e.scalar_tensor_tensor(b_[:, sh:TTL], src[:, 0:TTL - sh], w, a_[:, sh:TTL], ALU.mult, ALU.add),
                               deps=[tconv, b.f("cvs1")], inc="V")
                if tt == 0:
                    thead = b.emit("vector", lambda e, sh=sh, a_=a_, b_=b_: e.tensor_copy(b_[:, 0:sh], a_[:, 0:sh]),
                                   deps=[tmain], inc="V")
                else:
                    thead = b.emit("vector", lambda e, sh=sh, w=wc[dd], f=f, a_=a_, b_=b_:
                                   e.scalar_tensor_tensor(b_[:, 0:sh], b.ccar[f][:, 3 - sh:3], w, a_[:, 0:sh], ALU.mult, ALU.add),
                                   deps=[tmain, b.f(("ccar", f))], inc="V")
                b.setf("cvs1", thead)
                a_, b_ = b_, a_
                tconv = thead
            tcar = None
            if tt < NTT - 1:
                tcar = b.emit("vector", lambda e, src=src, f=f: e.tensor_copy(b.ccar[f][:, 0:3], src[:, TTL - 3:TTL]),
                              deps=[xc_tok[f], b.f(("ccar_r", f))], inc="V")
                b.setf(("ccar", f), tcar)
            tso = b.emit("scalar", lambda e, src=src, f=f, a_=a_:
                         e.activation(src[:, :], a_[:, :], AF.Silu, bias=cbS[:, f:f + 1]),
                         deps=[tconv, ld_cb] + ([tcar] if tcar else []), inc="S")
            b.setf("cvs0", tso)
            b.setf("cvs1", tso)
            xc_tok[f] = tso

        if debug and tt == 0:
            for f in range(17):
                xc_tok[f] = b.dma("gpsimd", DXC[f * 128:(f + 1) * 128, :], xc[f][:, :], deps=[xc_tok[f]], sem="ST")
            dt_tok = b.dma("gpsimd", DDT, dt_tt[:, :], deps=[dt_tok], sem="ST")

        for j in range(NCH_TT):
            cs = slice(j * CH, (j + 1) * CH)
            ttl = None
            for f in range(17):
                tp = b.emit("tensor", lambda e, f=f, cs=cs: e.transpose(b.psb[2][0:64, 0:128], xc[f][:, cs], identS[:, :]),
                            deps=[xc_tok[f], ld_id, b.bankdep(2)], inc="T")
                ttl = b.emit("vector", lambda e, f=f: e.tensor_copy(xtm[:, f * 128:(f + 1) * 128], b.psb[2][0:64, 0:128]),
                            deps=[tp, b.f("xtm")], inc="V")
                b.bankdone(2, ttl)
            xtm_tok = ttl
            b.setf("xtm", xtm_tok)

            tp = b.emit("tensor", lambda e, cs=cs: e.transpose(b.psb[5][0:64, 0:32], dtA_tt[:, cs], identS[0:32, 0:32]),
                        deps=[dt_tok, b.bankdep(5)], inc="T")
            ev = b.emit("vector", lambda e: e.tensor_copy(acs[:, :], b.psb[5][0:64, 0:32]),
                        deps=[tp, b.f("acs")], inc="V")
            b.bankdone(5, ev)
            mmc = b.emit("tensor", lambda e: e.matmul(b.psb[5][0:64, 0:32], triuS[:, :], acs[:, :], start=True, stop=True),
                         deps=[ev, ld_tr, b.bankdep(5)], inc="T")
            ev2 = b.emit("vector", lambda e: e.tensor_copy(acs[:, :], b.psb[5][0:64, 0:32]),
                         deps=[mmc], inc="V")
            b.bankdone(5, ev2)
            b.setf("acs", ev2)
            tp2 = b.emit("tensor", lambda e: e.transpose(b.psb[5][0:32, 0:64], acs[:, :], identS[0:64, 0:64]),
                         deps=[ev2, b.bankdep(5)], inc="T")
            af = b.emit("vector", lambda e: e.tensor_copy(acf[:, :], b.psb[5][0:32, 0:64]),
                        deps=[tp2, b.f("acf")], inc="V")
            e2 = b.emit("scalar", lambda e: e.activation(eplus_f[:, :], b.psb[5][0:32, 0:64], AF.Exp),
                        deps=[tp2, b.f("eplus_f")], inc="S")
            b.bankdone(5, e2)
            nf_t = b.emit("vector", lambda e: e.tensor_scalar(nacf[:, :], acf[:, :], -1.0, None, ALU.mult),
                          deps=[af, b.f("nacf")], inc="V")
            dd1 = b.emit("vector", lambda e: e.tensor_scalar(ddc[:, :], acf[:, :], acf[:, 63:64], None, ALU.subtract),
                         deps=[af, b.f("ddc")], inc="V")
            dd2 = b.emit("scalar", lambda e, m1c=m1c: e.activation(ddc[:, :], ddc[:, :], AF.Exp, scale=m1c[0:32, :]),
                         deps=[dd1, m1t], inc="S")
            b.setf("acf", dd2)

            if debug and tt == 0 and j == 0:
                a_d = b.dma("gpsimd", DAC, acs[:, :], deps=[b.f("acs")], sem="ST")
                b.setf("acs", a_d)
                e_d = b.dma("gpsimd", DEP, eplus_f[:, :], deps=[e2], sem="ST")
                b.setf("eplus_f", e_d)
                e2 = e_d
            cft = b.emit("vector", lambda e, cs=cs: e.tensor_copy(cfm0[:, :], xc[16][64:128, cs]),
                         deps=[xc_tok[16], b.f("cfm0")], inc="V")

            lt_toks = []
            for q in range(4):
                qs = slice(q * FT, (q + 1) * FT)
                mm1 = b.emit("tensor", lambda e, qs=qs: e.matmul(b.psb[3][0:64, :], nacf[:, :], r64S[:, qs], start=True, stop=False),
                             deps=[nf_t, ld_r64, b.bankdep(3)], inc=None)
                tt3 = b.emit("vector", lambda e, qs=qs: e.tensor_tensor(
                    b.sq[0:32, 0:FT],
                    r64S[:, qs].rearrange("p (q l) -> p q l", l=64),
                    acf[:, :].rearrange("p (q l) -> p q l", q=1).broadcast_to([32, FT // 64, 64]),
                    ALU.mult), deps=[af, ld_r64, b.f("sq")], inc="V")
                mm2 = b.emit("tensor", lambda e: e.matmul(b.psb[3][0:64, :], ones3264[:, :], b.sq[0:32, 0:FT], start=False, stop=True),
                             deps=[tt3, i2], inc="T")
                b.setf("sq", mm2)
                lt1 = b.emit("vector", lambda e, qs=qs: e.tensor_scalar(LTs[:, qs], b.psb[3][0:64, :], 0.0, None, ALU.min),
                             deps=[mm2, b.f("LTs")], inc="V")
                b.bankdone(3, lt1)
                lt_toks.append(lt1)
            lt2 = b.emit("scalar", lambda e: e.activation(LTs[:, :], LTs[:, :], AF.Exp),
                         deps=lt_toks, inc="S")
            lt3 = b.emit("vector", lambda e: e.tensor_tensor(LTs[:, :], LTs[:, :], triuRTS[:, :], ALU.mult),
                         deps=[lt2, ld_trt], inc="V")
            glt = None
            for q in range(4):
                qs = slice(q * FT, (q + 1) * FT)
                mmG = b.emit("tensor", lambda e, qs=qs, cs=cs: e.matmul(
                    b.psb[3][0:64, :], xc[16][0:64, cs],
                    cfm0[:, :].rearrange("p (q l) -> p q l", q=1).broadcast_to([64, FT // 64, 64]),
                    start=True, stop=True),
                    deps=[xc_tok[16], cft, b.bankdep(3)], inc="T")
                glt = b.emit("vector", lambda e, qs=qs: e.tensor_tensor(LTs[:, qs], LTs[:, qs], b.psb[3][0:64, :], ALU.mult),
                             deps=[mmG, lt3], inc="V")
                b.bankdone(3, glt)
            b.setf("LTs", glt)

            ets_tok = None
            for q in range(4):
                qs = slice(q * FT, (q + 1) * FT)
                mmE = b.emit("tensor", lambda e, qs=qs: e.matmul(b.psb[3][0:64, :], eplus_f[:, :], r64S[:, qs], start=True, stop=True),
                             deps=[e2, ld_r64, b.bankdep(3)], inc="T")
                ets_tok = b.emit("vector", lambda e, qs=qs: e.tensor_copy(ets[:, qs], b.psb[3][0:64, :]),
                                 deps=[mmE, b.f("ets")], inc="V")
                b.bankdone(3, ets_tok)
            b.setf("ets", ets_tok)

            xtl_tok = None
            for q in range(4):
                qs = slice(q * FT, (q + 1) * FT)
                mmD = b.emit("tensor", lambda e, qs=qs, cs=cs: e.matmul(b.psb[3][0:64, :], dt_tt[:, cs], r64S[:, qs], start=True, stop=True),
                             deps=[dt_tok, ld_r64, b.bankdep(3)], inc="T")
                xtl_tok = b.emit("vector", lambda e, qs=qs: e.tensor_tensor(xtl[:, qs], xtm[:, qs], b.psb[3][0:64, :], ALU.mult),
                                 deps=[mmD, xtm_tok, b.f("xtl")], inc="V")
                b.bankdone(3, xtl_tok)
            b.setf("xtl", xtl_tok)

            if debug and tt == 0 and j == 0:
                xl_ = b.dma("gpsimd", DXL, xtl[:, :], deps=[xtl_tok], sem="ST")
                b.setf("xtl", xl_); xtl_tok = xl_
                ll_ = b.dma("gpsimd", DLT, LTs[:, :], deps=[glt], sem="ST")
                b.setf("LTs", ll_); glt = ll_
                my0 = b.emit("tensor", lambda e: e.matmul(b.psb[6][0:64, 0:64], LTs[:, 0:64], xtl[:, 0:64], start=True, stop=True),
                             deps=[glt, xtl_tok, b.bankdep(6)], inc="T")
                ev0 = b.emit("vector", lambda e: e.tensor_copy(b.sq[0:64, 0:64], b.psb[6][0:64, 0:64]),
                             deps=[my0, b.f("sq")], inc="V")
                b.bankdone(6, ev0)
                y0_ = b.dma("gpsimd", DY0, b.sq[0:64, 0:64], deps=[ev0], sem="ST")
                b.setf("sq", y0_)
            Scur = Sping[chunk_idx % 2]
            Snxt = Sping[1 - chunk_idx % 2]
            sc_tok = b.f("Scur") if chunk_idx > 0 else iS
            prev_Sread = b.f("S_read_prev")
            y_toks = []
            for hg in range(4):
                ybank = hg % 2
                hs = slice(hg * FT, (hg + 1) * FT)
                mmy = None
                mmo = None
                for hh in range(8):
                    h = hg * 8 + hh
                    hsl = slice(h * 64, (h + 1) * 64)
                    lsl = slice(h * 64, (h + 1) * 64)
                    osl = slice(hh * 64, (hh + 1) * 64)
                    mmy = b.emit("tensor", lambda e, hsl=hsl, lsl=lsl, osl=osl, yb=ybank:
                                 e.matmul(b.psb[yb][0:64, osl], LTs[:, lsl], xtl[:, hsl], start=True, stop=False),
                                 deps=[glt, xtl_tok, b.bankdep(ybank) if hh == 0 else None],
                                 inc="T" if hh == 7 else None)
                    mmo = b.emit("tensor", lambda e, hsl=hsl, osl=osl, S=Scur, st=(hh == 0):
                                 e.matmul(b.psb[5][0:64, osl], cfm0[:, :], S[:, hsl], start=st, stop=True),
                                 deps=[sc_tok, cft, b.bankdep(5) if hh == 0 else None],
                                 inc="T" if hh == 7 else None)
                ye1 = b.emit("vector", lambda e, hs=hs: e.tensor_tensor(b.sq[0:64, 0:FT], b.psb[5][0:64, :], ets[:, hs], ALU.mult),
                             deps=[mmo, ets_tok, b.f("sq")], inc="V")
                b.bankdone(5, ye1)
                ye2 = b.emit("vector", lambda e, hs=hs, yb=ybank: e.tensor_tensor(ytm[:, hs], b.psb[yb][0:64, :], b.sq[0:64, 0:FT], ALU.add),
                             deps=[mmy, ye1, b.f("ytm")], inc="V")
                b.bankdone(ybank, ye2)
                b.setf("sq", ye2)
                y_toks.append(ye2)
            b.setf("ytm", y_toks[-1])

            x2_tok = None
            for q in range(4):
                qs = slice(q * FT, (q + 1) * FT)
                mmDc = b.emit("tensor", lambda e, qs=qs: e.matmul(b.psb[3][0:64, :], ddc[:, :], r64S[:, qs], start=True, stop=True),
                              deps=[dd2, ld_r64, b.bankdep(3)], inc="T")
                x2_tok = b.emit("vector", lambda e, qs=qs: e.tensor_tensor(xtl[:, qs], xtl[:, qs], b.psb[3][0:64, :], ALU.mult),
                               deps=[mmDc, y_toks[-1]], inc="V")
                b.bankdone(3, x2_tok)
            b.setf("ddc", x2_tok)
            st_toks = []
            for hg in range(4):
                hs = slice(hg * FT, (hg + 1) * FT)
                mms = None
                for hh in range(8):
                    h = hg * 8 + hh
                    hsl = slice(h * 64, (h + 1) * 64)
                    osl = slice(hh * 64, (hh + 1) * 64)
                    mms = b.emit("tensor", lambda e, hsl=hsl, osl=osl:
                                 e.matmul(b.psb[4][0:64, osl], xtm[:, 2048:2112], xtl[:, hsl], start=True, stop=True),
                                 deps=[xtm_tok, x2_tok, b.bankdep(4) if hh == 0 else None],
                                 inc="T" if hh == 7 else None)
                sev = b.emit("vector", lambda e, hs=hs: e.tensor_copy(stc[:, hs], b.psb[4][0:64, :]),
                             deps=[mms, b.f("stc")], inc="V")
                b.bankdone(4, sev)
                st_toks.append(sev)
            b.setf("stc", st_toks[-1])
            b.setf("xtl", st_toks[-1])

            lam_t = b.emit("vector", lambda e: e.tensor_scalar(lamR[:, :], r64S[:, :], eplus_f[:, 63:64], None, ALU.mult),
                           deps=[e2, ld_r64, b.f("lamR"), y_toks[-1]], inc="V")
            b.setf("eplus_f", lam_t)
            sup = None
            for q in range(4):
                qs = slice(q * FT, (q + 1) * FT)
                mmL = b.emit("tensor", lambda e, qs=qs: e.matmul(b.psb[3][0:64, :], ones3264[:, :], lamR[:, qs], start=True, stop=True),
                             deps=[lam_t, i2, b.bankdep(3)], inc="T")
                s1 = b.emit("vector", lambda e, qs=qs, S=Scur: e.tensor_tensor(b.sq[0:64, 0:FT], S[:, qs], b.psb[3][0:64, :], ALU.mult),
                            deps=[mmL, sc_tok, b.f("sq"), prev_Sread], inc="V")
                b.bankdone(3, s1)
                sup = b.emit("vector", lambda e, qs=qs, S=Snxt: e.tensor_tensor(S[:, qs], b.sq[0:64, 0:FT], stc[:, qs], ALU.add),
                             deps=[s1, st_toks[-1], prev_Sread], inc="V")
                b.setf("sq", sup)
            b.setf("Scur", sup)
            b.setf("cfm0", y_toks[-1])
            b.setf("stc", sup)
            b.setf("lamR", sup)
            b.setf("S_read_prev", y_toks[-1])

            for f in range(16):
                tp = b.emit("tensor", lambda e, f=f: e.transpose(b.psb[2][0:128, 0:64], ytm[:, f * 128:(f + 1) * 128], identS[0:64, 0:64]),
                            deps=[y_toks[(f * 2) // 8], y_toks[(f * 2 + 1) // 8], b.bankdep(2)], inc="T")
                ev = b.emit("vector", lambda e, f=f, cs=cs:
                            e.scalar_tensor_tensor(xc[f][:, cs], xc[f][:, cs], dcolS[f][:, 0:1], b.psb[2][0:128, 0:64], ALU.mult, ALU.add),
                            deps=[tp, dcol_t[f], xc_tok[f]], inc="V")
                b.bankdone(2, ev)
                xc_tok[f] = ev
            b.setf("cvs0", b.f("xtl"))
            b.setf("cvs1", b.f("xtl"))
            chunk_idx += 1

        sil_last = None
        for f in range(16):
            sv = b.emit("scalar", lambda e, f=f: e.activation(cvs[0][:, :], zt[f][:, :], AF.Silu),
                        deps=[zt_tok[f], b.f("cvs0")], inc="S")
            gv = b.emit("vector", lambda e, f=f: e.tensor_tensor(xc[f][:, :], xc[f][:, :], cvs[0][:, :], ALU.mult),
                        deps=[sv, xc_tok[f]], inc="V")
            b.setf("cvs0", gv)
            xc_tok[f] = gv
            b.setf(("zt", f), gv)
            sil_last = gv
        if debug and tt == 0:
            for f in range(16):
                xc_tok[f] = b.dma("gpsimd", DG[f * 128:(f + 1) * 128, :], xc[f][:, :], deps=[xc_tok[f]], sem="ST")
        mm = None
        for f in range(16):
            sqt = b.emit("scalar", lambda e, f=f: e.activation(b.sq[:, 0:FT], xc[f][:, :], AF.Square),
                         deps=[xc_tok[f], b.f("sq")], inc="S")
            mm = b.emit("tensor", lambda e, st=(f == 0), sp=(f == 15):
                        e.matmul(b.psb[6][0:1, :], b.ones_col[:, 0:1], b.sq[:, 0:FT], start=st, stop=sp),
                        deps=[sqt, b.bankdep(6)], inc="T")
            b.setf("sq", mm)
        ev = b.emit("vector", lambda e: e.tensor_copy(grs[0:1, :], b.psb[6][0:1, 0:FT]), deps=[mm, b.f("grs")], inc="V")
        b.bankdone(6, ev)
        t = b.emit("vector", lambda e: e.tensor_scalar(grs[0:1, :], grs[0:1, :], 1.0 / 2048.0, None, ALU.mult), deps=[ev], inc="V")
        b.setf("grs", t)
        epc, ept = b.cval(EPS)
        nhc, nht = b.cval(-0.5)
        t = b.emit("scalar", lambda e, epc=epc: e.activation(gr2[0:1, :], grs[0:1, :], AF.Ln, bias=epc[0:1, :]), deps=[t, ept, b.f("gr2")], inc="S")
        rstd_t = b.emit("scalar", lambda e, nhc=nhc: e.activation(gr2[0:1, :], gr2[0:1, :], AF.Exp, scale=nhc[0:1, :]), deps=[t, nht], inc="S")
        mmr = b.emit("tensor", lambda e: e.matmul(b.psb[6][:, :], b.ones_r128[0:1, :], gr2[0:1, :], start=True, stop=True),
                     deps=[rstd_t, b.bankdep(6)], inc="T")
        gb_toks = []
        for f in range(16):
            g1 = b.emit("vector", lambda e, f=f: e.tensor_tensor(cvs[1][:, :], xc[f][:, :], b.psb[6][:, :], ALU.mult),
                        deps=[mmr, xc_tok[f], b.f("cvs1")], inc="V")
            g2 = b.emit("vector", lambda e, f=f: e.tensor_scalar(gbf[f][:, :], cvs[1][:, :], nwS[f][:, 0:1], None, ALU.mult),
                        deps=[g1, nw_t[f], b.f(("gbf", f))], inc="V")
            b.setf("cvs1", g2)
            b.setf(("gbf", f), g2)
            gb_toks.append(g2)
            b.setf(("xc", f), g1)
        b.bankdone(6, gb_toks[-1])
        b.setf("gr2", gb_toks[-1])
        b.setf("xtl", gb_toks[-1])
        b.setf(("xc", 16), sil_last)

        def wout_tile_loads(m, bi):
            toks = []
            for kc in range(16):
                toks.append(b.dma("gpsimd", wbuf[bi][kc][:, :],
                                  WOUT[kc * 128:(kc + 1) * 128, m * 128:(m + 1) * 128],
                                  deps=[b.f(("wb", bi, kc))], sem=f"WD{bi}"))
            return toks
        wtoks = wout_tile_loads(0, 0)
        for m in range(DC):
            bi = m % 2
            nwt = wout_tile_loads(m + 1, 1 - bi) if m + 1 < DC else None
            mm = None
            for kc in range(16):
                mm = b.emit("tensor",
                            lambda e, o=b.psb[bi], l=wbuf[bi][kc], r=gbf[kc],
                            st=(kc == 0), sp=(kc == 15):
                            e.matmul(o[:, :], l[:, :], r[:, :], start=st, stop=sp),
                            deps=[wtoks[kc], gb_toks[kc], b.bankdep(bi)],
                            inc="T" if kc == 15 else None)
            ev = b.emit("vector", lambda e, p=b.psb[bi]: e.tensor_copy(ostg[:, :], p[:, :]),
                        deps=[mm, b.f("ostg")], inc="V")
            b.bankdone(bi, ev)
            std = b.dma("gpsimd", OCD[m * 128:(m + 1) * 128, t0:t0 + TTL], ostg[:, :], deps=[ev], sem="ST")
            b.setf("ostg", std)
            ocd_last = std
            for kc in range(16):
                b.setf(("wb", bi, kc), mm)
            wtoks = nwt
        b.setf("xtm", ocd_last)
        b.setf("ytm", ocd_last)
        for f in range(17):
            b.setf(("ccar_r", f), xc_tok[f])

    def stream_stats(SRC, a_row, b_row, extra_dep, scr_key):
        sdone = []
        for c in range(DC):
            bi = c % 2
            ld = b.dma("sync", ocs[bi][:, :], SRC[c * 128:(c + 1) * 128, :], deps=[b.f(("ocs", 0)), b.f(("ocs", 1)), extra_dep])
            cons = []
            for ft in range(NF):
                fs = slice(ft * FT, (ft + 1) * FT)
                mm1 = b.emit("tensor", lambda e, x=ocs[bi], fs=fs, bk=2 + ft, st=(c == 0), sp=(c == DC - 1):
                             e.matmul(b.psb[bk][0:1, :], b.ones_col[:, 0:1], x[:, fs], start=st, stop=sp),
                             deps=[ld, b.bankdep(2 + ft) if c == 0 else None], inc="T")
                sqt = b.emit("scalar", lambda e, x=ocs[bi], fs=fs: e.activation(b.sq[:, 0:FT], x[:, fs], AF.Square),
                             deps=[ld, b.f("sq")], inc="S")
                mm2 = b.emit("tensor", lambda e, bk=2 + ft, st=(c == 0), sp=(c == DC - 1):
                             e.matmul(b.psb[bk][64:65, :], b.ones_col[:, 0:1], b.sq[:, 0:FT], start=st, stop=sp),
                             deps=[sqt], inc="T")
                b.setf("sq", mm2)
                cons += [mm1, mm2]
            b.setf(("ocs", bi), cons[-1])
            sdone = cons
        evs = []
        for ft in range(NF):
            fs = slice(ft * FT, (ft + 1) * FT)
            ea = b.emit("vector", lambda e, fs=fs, bk=2 + ft: e.tensor_copy(a_row[0:1, fs], b.psb[bk][0:1, 0:FT]),
                        deps=sdone, inc="V")
            eb = b.emit("vector", lambda e, fs=fs, bk=2 + ft: e.tensor_copy(b_row[0:1, fs], b.psb[bk][64:65, 0:FT]),
                        deps=sdone, inc="V")
            b.bankdone(2 + ft, eb)
            evs += [ea, eb]
        return ln_row_math(b, a_row, b_row, rt2, 1.0 / D, evs, scr_key)

    st2 = stream_stats(OCD, a2, b2, ocd_last, "rt2")

    r3done = []
    for c in range(DC):
        bi = c % 2
        ldo = b.dma("sync", ocs[bi][:, :], OCD[c * 128:(c + 1) * 128, :], deps=[b.f(("ocs", 0)), b.f(("ocs", 1)), st2])
        ldr = b.dma("sync", r0c[:, :], R2i[c * 128:(c + 1) * 128, :], deps=[b.f("r0c")])
        lt = ln_apply(b, zc2, ocs[bi], ldo, a2, b2, st2, 6, 7, out_war=b.f("zc2"))
        ad = b.emit("vector", lambda e: e.tensor_tensor(xlnc[:, :], r0c[:, :], zc2[:, :], ALU.add),
                    deps=[lt, ldr, b.f("xlnc")], inc="V")
        b.setf("r0c", ad)
        b.setf("zc2", ad)
        b.setf(("ocs", bi), lt)
        stR = b.dma("gpsimd", R3o[c * 128:(c + 1) * 128, :], xlnc[:, :], deps=[ad], sem="ST")
        cons = []
        for ft in range(NF):
            fs = slice(ft * FT, (ft + 1) * FT)
            mm1 = b.emit("tensor", lambda e, fs=fs, bk=2 + ft, st=(c == 0), sp=(c == DC - 1):
                         e.matmul(b.psb[bk][0:1, :], b.ones_col[:, 0:1], xlnc[:, fs], start=st, stop=sp),
                         deps=[ad, b.bankdep(2 + ft) if c == 0 else None], inc="T")
            sqt = b.emit("scalar", lambda e, fs=fs: e.activation(b.sq[:, 0:FT], xlnc[:, fs], AF.Square),
                         deps=[ad, b.f("sq")], inc="S")
            mm2 = b.emit("tensor", lambda e, bk=2 + ft, st=(c == 0), sp=(c == DC - 1):
                         e.matmul(b.psb[bk][64:65, :], b.ones_col[:, 0:1], b.sq[:, 0:FT], start=st, stop=sp),
                         deps=[sqt], inc="T")
            b.setf("sq", mm2)
            cons += [mm1, mm2]
        b.setf("xlnc", cons[-1] if cons else stR)
        r3done = cons
    evs = []
    for ft in range(NF):
        fs = slice(ft * FT, (ft + 1) * FT)
        ea = b.emit("vector", lambda e, fs=fs, bk=2 + ft: e.tensor_copy(a3[0:1, fs], b.psb[bk][0:1, 0:FT]),
                    deps=r3done + [b.f("a3")], inc="V")
        eb = b.emit("vector", lambda e, fs=fs, bk=2 + ft: e.tensor_copy(b3[0:1, fs], b.psb[bk][64:65, 0:FT]),
                    deps=r3done, inc="V")
        b.bankdone(2 + ft, eb)
        evs += [ea, eb]
    st3 = ln_row_math(b, a3, b3, rt2, 1.0 / D, evs, "rt2")

    for c in range(DC):
        bi = c % 2
        ldo = b.dma("sync", ocs[bi][:, :], OCD[c * 128:(c + 1) * 128, :], deps=[b.f(("ocs", 0)), b.f(("ocs", 1)), st3])
        ldr = b.dma("sync", r0c[:, :], R2i[c * 128:(c + 1) * 128, :], deps=[b.f("r0c")])
        lt = ln_apply(b, zc2, ocs[bi], ldo, a2, b2, st2, 6, 7, out_war=b.f("zc2"))
        ad = b.emit("vector", lambda e: e.tensor_tensor(xlnc[:, :], r0c[:, :], zc2[:, :], ALU.add),
                    deps=[lt, ldr, b.f("xlnc")], inc="V")
        b.setf("r0c", ad)
        b.setf("zc2", ad)
        b.setf(("ocs", bi), lt)
        lt2 = ln_apply(b, zc2, xlnc, ad, a3, b3, st3, 6, 7, out_war=ad)
        stx = b.dma("gpsimd", XLN3o[c * 128:(c + 1) * 128, :], zc2[:, :], deps=[lt2], sem="ST")
        b.setf("zc2", stx)
        b.setf("xlnc", lt2)
    b.emit("gpsimd", lambda e: e.nop(), deps=[("ST", b.semn["ST"])])
    return b





def build_launch5c():
    b = PB(arena_kb=200)
    XLN = b.din("xln3", [D, T])
    R3i = b.din("r3", [D, T])
    W1T = b.din("w1T", [D, 4 * D])
    W2T = b.din("w2T", [D, 4 * D])
    H1T = b.din("h1T", [4 * D, 4 * D])
    H2T = b.din("h2T", [4 * D, 4 * D])
    W3T = b.din("w3T", [4 * D, D])
    S1C = b.din("s1c", [4 * D, 1])
    S2C = b.din("s2c", [4 * D, 1])
    SH1C = b.din("sh1c", [4 * D, 1])
    SH2C = b.din("sh2c", [4 * D, 1])
    S3C = b.din("s3c", [D, 1])
    OUTo = b.dout("OUT", [D, T])
    OCD = b.nc.dram_tensor("ocd5c", [D, T], F32, kind="Internal").ap()

    NH4 = 4 * DC
    HT = T // 2

    off = 0
    def fx(name, shape, dt):
        nonlocal off
        h = b.place(name, shape, dt, off)
        off = (off + int(np.prod(shape[1:])) * mybir.dt.size(dt) + 31) // 32 * 32
        return h
    b.ones_col = fx("ones_col", [128, 1], F32)
    b.ones_r128 = fx("ones_r128", [1, 128], F32)
    b.mrow = fx("mrow", [1, 128], F32)
    b.make_cpool(fx)
    b.sq = fx("sq", [128, FT], F32)
    b.lnt = fx("lnt", [128, FT], F32)
    s1S = fx("s1S", [128, NH4], F32)
    s2S = fx("s2S", [128, NH4], F32)
    sh1S = fx("sh1S", [128, NH4], F32)
    sh2S = fx("sh2S", [128, NH4], F32)
    s3S = fx("s3S", [128, DC], F32)
    stg = [fx(f"stg{i}", [128, FT], F32) for i in range(2)]
    xq = [fx(f"xq{c}", [128, HT], BF16) for c in range(DC)]
    assert off <= 48 * KB, off
    P0 = 48 * KB
    ab = [b.place(f"ab{c}", [128, HT], BF16, P0 + c * 2 * KB) for c in range(NH4)]
    bb = [b.place(f"bb{c}", [128, HT], BF16, P0 + 64 * KB + c * 2 * KB) for c in range(NH4)]
    wbuf = [[b.place(f"wb{i}_{kc}", [128, 128], BF16, P0 + 128 * KB + (i * NH4 + kc) * 256)
             for kc in range(NH4)] for i in range(2)]
    r0c = b.place("r0c", [128, T], F32, P0 + 0)
    ocs = b.place("ocs", [128, T], F32, P0 + 8 * KB)
    zc2 = b.place("zc2", [128, T], F32, P0 + 16 * KB)
    a2 = b.place("a2", [1, T], F32, P0 + 24 * KB)
    b2 = b.place("b2", [1, T], F32, P0 + 32 * KB)
    rt2 = b.place("rt2", [1, T], F32, P0 + 40 * KB)
    b.psb = [b.psum(f"ps{i}") for i in range(8)]

    i0 = b.emit("vector", lambda e: e.memset(b.ones_col[:, :], 1.0), inc="V")
    i1 = b.emit("vector", lambda e: e.memset(b.ones_r128[0:1, :], 1.0), inc="V")
    sc_t = {}
    for name, SS, SRC, n in (("s1", s1S, S1C, NH4), ("s2", s2S, S2C, NH4),
                             ("sh1", sh1S, SH1C, NH4), ("sh2", sh2S, SH2C, NH4),
                             ("s3", s3S, S3C, DC)):
        for c in range(n):
            sc_t[(name, c)] = b.dma("sync", SS[:, c:c + 1], SRC[c * 128:(c + 1) * 128, :])

    def wload(WT_ap, kcs, m, bi):
        toks = []
        for kc in range(kcs):
            toks.append(b.dma("gpsimd", wbuf[bi][kc][:, :],
                              WT_ap[kc * 128:(kc + 1) * 128, m * 128:(m + 1) * 128],
                              deps=[b.f(("wb", bi, kc))], sem=f"WD{bi}"))
        return toks

    class WSeq:
        def __init__(self):
            self.jobs = []
        def run(self):
            nj = len(self.jobs)
            wtoks = None
            for idx, (WT_ap, kcs, m, body) in enumerate(self.jobs):
                bi = idx % 2
                if idx == 0:
                    wtoks = wload(WT_ap, kcs, m, bi)
                nxt = self.jobs[idx + 1] if idx + 1 < nj else None
                nwt = wload(nxt[0], nxt[1], nxt[2], 1 - bi) if nxt else None
                lastmm = body(bi, wtoks)
                for kc in range(kcs):
                    b.setf(("wb", bi, kc), lastmm)
                wtoks = nwt

    ocd_last = None
    for hf in range(2):
        hofs = hf * HT
        xq_toks = []
        for c in range(DC):
            t = b.dma("gpsimd", xq[c][:, :], XLN[c * 128:(c + 1) * 128, hofs:hofs + HT],
                      deps=[b.f(("xq", c))], sem="LD")
            xq_toks.append(t)
        seq = WSeq()

        def mk_mm(rhs, rhs_deps, kcs, bank, m, fto):
            def go(bi, wtoks):
                mm = None
                for kc in range(kcs):
                    mm = b.emit("tensor",
                                lambda e, o=b.psb[bank], l=wbuf[bi][kc], r=rhs[kc],
                                fs=slice(fto * FT, (fto + 1) * FT), st=(kc == 0), sp=(kc == kcs - 1):
                                e.matmul(o[:, :], l[:, :], r[:, fs], start=st, stop=sp),
                                deps=[wtoks[kc], rhs_deps[kc], b.bankdep(bank)],
                                inc="T" if kc == kcs - 1 else None)
                return mm
            return go

        a_toks = [None] * NH4
        for m in range(NH4):
            for fto in range(2):
                bank = fto % 2
                g1 = mk_mm(xq, xq_toks, DC, bank, m, fto)
                g2 = mk_mm(xq, xq_toks, DC, 2 + bank, m, fto)
                def bodyA(bi, wtoks, m=m, fto=fto, bank=bank, g1=g1):
                    mm = g1(bi, wtoks)
                    ev = b.emit("scalar", lambda e, m=m, p=b.psb[bank]:
                                e.activation(b.sq[:, 0:FT], p[:, :], AF.Silu, scale=s1S[:, m:m + 1]),
                                deps=[mm, sc_t[("s1", m)], b.f("sq")], inc="S")
                    b.bankdone(bank, ev)
                    b.setf("sq", ev)
                    b.setf(("evA", m, fto), ev)
                    return mm
                def bodyB(bi, wtoks, m=m, fto=fto, bank=bank, g2=g2):
                    mm = g2(bi, wtoks)
                    ev1 = b.emit("vector", lambda e, m=m, p=b.psb[2 + bank], fto=fto:
                                 e.tensor_scalar(b.lnt[:, 0:FT], p[:, :], s2S[:, m:m + 1], None, ALU.mult),
                                 deps=[mm, sc_t[("s2", m)], b.f("lnt")], inc="V")
                    b.bankdone(2 + bank, ev1)
                    pv = b.emit("vector", lambda e, m=m, fto=fto, o=ab[m]:
                                e.tensor_tensor(o[:, fto * FT:(fto + 1) * FT], b.sq[:, 0:FT], b.lnt[:, 0:FT], ALU.mult),
                                deps=[ev1, b.f(("evA", m, fto)), b.f(("ab", m))], inc="V")
                    b.setf("lnt", pv)
                    b.setf("sq", pv)
                    b.setf(("ab", m), pv)
                    a_toks[m] = pv
                    return mm
                seq.jobs.append((W1T, DC, m, bodyA))
                seq.jobs.append((W2T, DC, m, bodyB))
        seq.run()

        seq = WSeq()
        b_toks = [None] * NH4
        for m in range(NH4):
            for fto in range(2):
                bank = fto % 2
                g = mk_mm(ab, a_toks, NH4, bank, m, fto)
                def bodyH(bi, wtoks, m=m, fto=fto, bank=bank, g=g):
                    mm = g(bi, wtoks)
                    ev = b.emit("scalar", lambda e, m=m, fto=fto, p=b.psb[bank], o=bb[m]:
                                e.activation(o[:, fto * FT:(fto + 1) * FT], p[:, :], AF.Gelu, scale=sh1S[:, m:m + 1]),
                                deps=[mm, sc_t[("sh1", m)], b.f(("bb", m))], inc="S")
                    b.bankdone(bank, ev)
                    b.setf(("bb", m), ev)
                    b_toks[m] = ev
                    return mm
                seq.jobs.append((H1T, NH4, m, bodyH))
        seq.run()

        seq = WSeq()
        a2_toks = [None] * NH4
        for m in range(NH4):
            for fto in range(2):
                bank = fto % 2
                g = mk_mm(bb, b_toks, NH4, bank, m, fto)
                def bodyH2(bi, wtoks, m=m, fto=fto, bank=bank, g=g):
                    mm = g(bi, wtoks)
                    ev = b.emit("scalar", lambda e, m=m, fto=fto, p=b.psb[bank], o=ab[m]:
                                e.activation(o[:, fto * FT:(fto + 1) * FT], p[:, :], AF.Gelu, scale=sh2S[:, m:m + 1]),
                                deps=[mm, sc_t[("sh2", m)], b.f(("ab", m))], inc="S")
                    b.bankdone(bank, ev)
                    b.setf(("ab", m), ev)
                    a2_toks[m] = ev
                    return mm
                seq.jobs.append((H2T, NH4, m, bodyH2))
        seq.run()

        seq = WSeq()
        for m in range(DC):
            for fto in range(2):
                bank = fto % 2
                g = mk_mm(ab, a2_toks, NH4, bank, m, fto)
                def bodyO(bi, wtoks, m=m, fto=fto, bank=bank, g=g):
                    mm = g(bi, wtoks)
                    ev = b.emit("vector", lambda e, m=m, fto=fto, p=b.psb[bank], sgi=fto % 2:
                                e.tensor_scalar(stg[sgi][:, 0:FT], p[:, :], s3S[:, m:m + 1], None, ALU.mult),
                                deps=[mm, sc_t[("s3", m)], b.f(("stg", fto % 2))], inc="V")
                    b.bankdone(bank, ev)
                    st = b.dma("gpsimd", OCD[m * 128:(m + 1) * 128, hofs + fto * FT:hofs + (fto + 1) * FT],
                               stg[fto % 2][:, 0:FT], deps=[ev], sem="ST")
                    b.setf(("stg", fto % 2), st)
                    return mm
                seq.jobs.append((W3T, NH4, m, bodyO))
        seq.run()
        ocd_last = ("ST", b.semn["ST"])
        tall = ("T", b.semn["T"])
        for c in range(DC):
            b.setf(("xq", c), tall)
        for m in range(4 * DC):
            b.setf(("ab", m), tall)
            b.setf(("bb", m), tall)

    def stream_stats5c(SRC, a_row, b_row, extra_dep, scr_key):
        sdone = []
        for c in range(DC):
            ld = b.dma("sync", ocs[:, :], SRC[c * 128:(c + 1) * 128, :], deps=[b.f("ocs"), extra_dep])
            cons = []
            for ft in range(NF):
                fs = slice(ft * FT, (ft + 1) * FT)
                mm1 = b.emit("tensor", lambda e, fs=fs, bk=2 + ft, st=(c == 0), sp=(c == DC - 1):
                             e.matmul(b.psb[bk][0:1, :], b.ones_col[:, 0:1], ocs[:, fs], start=st, stop=sp),
                             deps=[ld, b.bankdep(2 + ft) if c == 0 else None], inc="T")
                sqt = b.emit("scalar", lambda e, fs=fs: e.activation(b.sq[:, 0:FT], ocs[:, fs], AF.Square),
                             deps=[ld, b.f("sq")], inc="S")
                mm2 = b.emit("tensor", lambda e, bk=2 + ft, st=(c == 0), sp=(c == DC - 1):
                             e.matmul(b.psb[bk][64:65, :], b.ones_col[:, 0:1], b.sq[:, 0:FT], start=st, stop=sp),
                             deps=[sqt], inc="T")
                b.setf("sq", mm2)
                cons += [mm1, mm2]
            b.setf("ocs", cons[-1])
            sdone = cons
        evs = []
        for ft in range(NF):
            fs = slice(ft * FT, (ft + 1) * FT)
            ea = b.emit("vector", lambda e, fs=fs, bk=2 + ft: e.tensor_copy(a_row[0:1, fs], b.psb[bk][0:1, 0:FT]),
                        deps=sdone + [b.f("rows5c")], inc="V")
            eb = b.emit("vector", lambda e, fs=fs, bk=2 + ft: e.tensor_copy(b_row[0:1, fs], b.psb[bk][64:65, 0:FT]),
                        deps=sdone, inc="V")
            b.bankdone(2 + ft, eb)
            evs += [ea, eb]
        return ln_row_math(b, a_row, b_row, rt2, 1.0 / D, evs, scr_key)

    st2 = stream_stats5c(OCD, a2, b2, ocd_last, "rt2")
    for c in range(DC):
        ldo = b.dma("sync", ocs[:, :], OCD[c * 128:(c + 1) * 128, :], deps=[b.f("ocs"), st2])
        ldr = b.dma("sync", r0c[:, :], R3i[c * 128:(c + 1) * 128, :], deps=[b.f("r0c")])
        lt = ln_apply(b, zc2, ocs, ldo, a2, b2, st2, 6, 7, out_war=b.f("zc2"))
        b.setf("ocs", lt)
        ad = b.emit("vector", lambda e: e.tensor_tensor(zc2[:, :], r0c[:, :], zc2[:, :], ALU.add),
                    deps=[lt, ldr], inc="V")
        b.setf("r0c", ad)
        stx = b.dma("gpsimd", OUTo[c * 128:(c + 1) * 128, :], zc2[:, :], deps=[ad], sem="ST")
        b.setf("zc2", stx)
    b.emit("gpsimd", lambda e: e.nop(), deps=[("ST", b.semn["ST"])])
    return b


def _prep_l1(d, core):
    return dict(
        x=np.ascontiguousarray(d["L0_attn_x"][core].T).astype(np.float32),
        wkT=np.ascontiguousarray(d["L0_attn_Wk"].T).astype(np.float32),
        wvT=np.ascontiguousarray(d["L0_attn_Wv"].T).astype(np.float32),
        wrT=np.ascontiguousarray(d["L0_attn_Wr"].T).astype(np.float32),
        woT=np.ascontiguousarray(d["L0_attn_Wo"].T).astype(np.float32),
        tmk=np.ascontiguousarray(d["L0_attn_time_mix_key"].reshape(D, 1)).astype(np.float32),
        tmv=np.ascontiguousarray(d["L0_attn_time_mix_value"].reshape(D, 1)).astype(np.float32),
        tmr=np.ascontiguousarray(d["L0_attn_time_mix_receptance"].reshape(D, 1)).astype(np.float32),
        tdec=np.ascontiguousarray(d["L0_attn_time_decay"].reshape(D, 1)).astype(np.float32),
        tfirst=np.ascontiguousarray(d["L0_attn_time_first"].reshape(D, 1)).astype(np.float32),
        boc=np.ascontiguousarray(d["L0_attn_bo"].reshape(D, 1)).astype(np.float32),
        ident=np.eye(128, dtype=np.float32),
    )


def run_launch1(d):
    b = build_launch1()
    nc = b.finalize()
    in_maps = [_prep_l1(d, c) for c in range(NCORES)]
    return run_bass_kernel_spmd(nc, in_maps, core_ids=list(range(NCORES)))


def kernel(**inputs):
    d = inputs
    PB.SERIAL = True
    ident = np.eye(128, dtype=np.float32)
    bmap = np.zeros((32, D), np.float32)
    for i in range(D):
        bmap[i // 32, i] = 1.0
    bmap96 = np.zeros((96, 3 * D), np.float32)
    for i in range(3 * D):
        bmap96[i // 32, i] = 1.0
    triu = np.triu(np.ones((64, 64), np.float32))
    triurt = np.zeros((64, 2048), np.float32)
    for h in range(32):
        triurt[:, h * 64:(h + 1) * 64] = triu
    r64 = np.zeros((32, 2048), np.float32)
    for h in range(32):
        r64[h, h * 64:(h + 1) * 64] = 1.0
    w4 = d["L1_attn_conv_w"][:, 0, :]
    cw = np.ascontiguousarray(w4.reshape(17, 128, 4).transpose(1, 0, 2).reshape(128, 68)).astype(np.float32)
    cb = np.ascontiguousarray(np.asarray(d["L1_attn_conv_b"]).reshape(17, 128).T).astype(np.float32)
    dcol = np.ascontiguousarray(np.repeat(np.asarray(d["L1_attn_D"]), 64).reshape(2048, 1)).astype(np.float32)
    nw = np.ascontiguousarray(np.asarray(d["L1_attn_norm_w"]).reshape(2048, 1)).astype(np.float32)

    def ct(a):
        return np.ascontiguousarray(np.asarray(a, dtype=np.float32))

    b = build_launch1()
    nc = b.finalize()
    res = run_bass_kernel_spmd(nc, [_prep_l1(d, c) for c in range(NCORES)], core_ids=list(range(NCORES)))
    R1 = [res.results[c]["R1"] for c in range(NCORES)]
    XLN1 = [res.results[c]["XLN1"] for c in range(NCORES)]
    m1a = np.ascontiguousarray(np.concatenate([res.results[c]["M1"] for c in range(NCORES)], axis=1))

    b = build_launch2()
    nc = b.finalize()
    wupT = ct(d["L0_ffn_w_up"].T); wgT = ct(d["L0_ffn_w_gate"].T)
    wupN = ct(d["L0_ffn_w_up"]); wgN = ct(d["L0_ffn_w_gate"])
    ins = [dict(xln=np.ascontiguousarray(XLN1[c]), m1a=m1a, wupT=wupT, wgT=wgT,
                wupN=wupN, wgN=wgN, bmap=bmap, ident=ident) for c in range(NCORES)]
    res = run_bass_kernel_spmd(nc, ins, core_ids=list(range(NCORES)))
    UP = [res.results[c]["UP"] for c in range(NCORES)]
    GATE = [res.results[c]["GATE"] for c in range(NCORES)]
    m2a = np.ascontiguousarray(np.concatenate([res.results[c]["M2"] for c in range(NCORES)], axis=1))
    m3a = np.ascontiguousarray(np.concatenate([res.results[c]["M3"] for c in range(NCORES)], axis=1))

    b = build_launch3()
    nc = b.finalize()
    ins = [dict(gate=np.ascontiguousarray(GATE[c]), m3a=m3a, ident=ident) for c in range(NCORES)]
    res = run_bass_kernel_spmd(nc, ins, core_ids=list(range(NCORES)))
    GS = [res.results[c]["GS"] for c in range(NCORES)]
    m4a = np.ascontiguousarray(np.concatenate([res.results[c]["M4"] for c in range(NCORES)], axis=1))

    b = build_launch4()
    nc = b.finalize()
    ins = [dict(gs=np.ascontiguousarray(GS[c]), up=np.ascontiguousarray(UP[c]),
                m4a=m4a, m2a=m2a, ident=ident) for c in range(NCORES)]
    res = run_bass_kernel_spmd(nc, ins, core_ids=list(range(NCORES)))
    OPRE = [res.results[c]["OPRE"] for c in range(NCORES)]
    m5a = np.ascontiguousarray(np.concatenate([res.results[c]["M5"] for c in range(NCORES)], axis=1))

    b = build_launch5a()
    nc = b.finalize()
    wdT = ct(d["L0_ffn_w_down"].T); wdN = ct(d["L0_ffn_w_down"])
    ins = [dict(opre=np.ascontiguousarray(OPRE[c]), m5a=m5a, r1=np.ascontiguousarray(R1[c]),
                wdT=wdT, wdN=wdN, bmap96=bmap96, ident=ident) for c in range(NCORES)]
    res = run_bass_kernel_spmd(nc, ins, core_ids=list(range(NCORES)))
    R2 = [res.results[c]["R2"] for c in range(NCORES)]
    XLN2 = [res.results[c]["XLN2"] for c in range(NCORES)]

    b = build_launch5b()
    nc = b.finalize()
    winT = ct(d["L1_attn_W_in"].T); woutT = ct(d["L1_attn_W_out"].T)
    ins = [dict(xln2=np.ascontiguousarray(XLN2[c]), r2=np.ascontiguousarray(R2[c]),
                winT=winT, woutT=woutT, cw=cw, cb=cb,
                dtb=ct(np.asarray(d["L1_attn_dt_bias"]).reshape(32, 1)),
                alog=ct(np.asarray(d["L1_attn_A_log"]).reshape(32, 1)),
                dcol=dcol, nw=nw, triu=triu, triurt=triurt, r64=r64, ident=ident)
           for c in range(NCORES)]
    res = run_bass_kernel_spmd(nc, ins, core_ids=list(range(NCORES)))
    R3 = [res.results[c]["R3"] for c in range(NCORES)]
    XLN3 = [res.results[c]["XLN3"] for c in range(NCORES)]

    b = build_launch5c()
    nc = b.finalize()
    ins = [dict(xln3=np.ascontiguousarray(XLN3[c]), r3=np.ascontiguousarray(R3[c]),
                w1T=ct(d["L1_ffn_w1_w"].T), w2T=ct(d["L1_ffn_w2_w"].T),
                h1T=ct(d["L1_ffn_h1_w"].T), h2T=ct(d["L1_ffn_h2_w"].T),
                w3T=ct(d["L1_ffn_w3_w"].T),
                s1c=ct(np.asarray(d["L1_ffn_w1_s"]).reshape(-1, 1)),
                s2c=ct(np.asarray(d["L1_ffn_w2_s"]).reshape(-1, 1)),
                sh1c=ct(np.asarray(d["L1_ffn_h1_s"]).reshape(-1, 1)),
                sh2c=ct(np.asarray(d["L1_ffn_h2_s"]).reshape(-1, 1)),
                s3c=ct(np.asarray(d["L1_ffn_w3_s"]).reshape(-1, 1)))
           for c in range(NCORES)]
    res = run_bass_kernel_spmd(nc, ins, core_ids=list(range(NCORES)))
    out = np.stack([res.results[c]["OUT"].T for c in range(NCORES)], axis=0)
    return np.ascontiguousarray(out)
